# version 05 (final)
# speedup vs baseline: 1.2625x; 1.0019x over previous
import math
import numpy as np
import ml_dtypes
import concourse.bass as bass
import concourse.mybir as mybir
from concourse.bass_utils import run_bass_kernel_spmd

F32 = mybir.dt.float32
BF16 = mybir.dt.bfloat16
U8 = mybir.dt.uint8
AF = mybir.ActivationFunctionType
ALU = mybir.AluOpType
AX = mybir.AxisListType
BF = ml_dtypes.bfloat16

T = 4096
D = 1024
NT = T // 128
ND = D // 128
TC = 512
NTC = T // TC

ENGINES = ("sync", "tensor", "vector", "scalar", "gpsimd")
SERIAL_LN = True


class Sem:
    __slots__ = ("h", "n")

    def __init__(self, nc, name):
        self.h = nc.alloc_semaphore(name)
        self.n = 0


class B:

    def __init__(self):
        self.nc = bass.Bass("TRN2", target_bir_lowering=False,
                            detect_race_conditions=False)
        self.ops = {e: [] for e in ENGINES}
        self._semi = 0
        self._sbi = 0
        self._sems = {}
        self.pool = {}

    def sem(self, name=None):
        self._semi += 1
        name = name or f"s{self._semi}"
        if name not in self._sems:
            self._sems[name] = Sem(self.nc, name)
        return self._sems[name]

    def wait(self, eng, sem, val=None):
        v = sem.n if val is None else val
        assert v <= sem.n, "wait on a future semaphore value"
        if v > 0:
            self.ops[eng].append(lambda e, h=sem.h, v=v: e.wait_ge(h, v))

    def op(self, eng, fn, sem=None, k=1):
        if sem is None:
            self.ops[eng].append(fn)
        else:
            sem.n += k
            self.ops[eng].append(
                lambda e, fn=fn, h=sem.h, k=k: fn(e).then_inc(h, k))

    def dma(self, eng, out, in_, sem=None, **kw):
        self.op(eng, lambda e, o=out, i=in_, kw=kw: e.dma_start(out=o, in_=i, **kw),
                sem, 16)

    def sb(self, name, shape, dtype):
        if name in self.pool:
            ap = self.pool[name]
            assert tuple(ap.shape) == tuple(shape) and ap.dtype == dtype, \
                f"pool mismatch {name}: {ap.shape}/{ap.dtype} vs {shape}/{dtype}"
            return ap
        ap = self.nc.alloc_sbuf_tensor(name, shape, dtype)[:]
        self.pool[name] = ap
        return ap

    def ps(self, name, shape, dtype=F32):
        if name in self.pool:
            return self.pool[name]
        ap = self.nc.alloc_psum_tensor(name, shape, dtype)[:]
        self.pool[name] = ap
        return ap

    def const(self, name, arr):
        d = self.nc.inline_tensor(np.ascontiguousarray(arr), name="c_" + name).ap()
        s = self.sb("sc_" + name, list(arr.shape), mybir.dt.bfloat16 if arr.dtype == BF
                    else (F32 if arr.dtype == np.float32 else U8))
        return d, s

    def barrier(self):
        s_ = self.sem("bar")
        for e_ in ENGINES:
            self.op(e_, lambda en: en.nop(), s_, 1)
        for e_ in ENGINES:
            self.wait(e_, s_)

    def finish(self):
        ops = self.ops
        with self.nc.Block() as block:
            @block.sync
            def _(e):
                for f in ops["sync"]:
                    f(e)

            @block.tensor
            def _(e):
                for f in ops["tensor"]:
                    f(e)

            @block.vector
            def _(e):
                for f in ops["vector"]:
                    f(e)

            @block.scalar
            def _(e):
                for f in ops["scalar"]:
                    f(e)

            @block.gpsimd
            def _(e):
                for f in ops["gpsimd"]:
                    f(e)
        return self.nc


def load_consts(b):
    if hasattr(b, "_consts"):
        return b._consts
    c = {"ld": b.sem("c_ld")}
    b._consts = c

    def add(name, arr):
        d, s = b.const(name, arr)
        b.dma("sync", s, d, c["ld"])
        c[name] = s

    add("ident", np.eye(128, dtype=BF))
    add("ident32", np.eye(128, dtype=np.float32))
    add("ones_r16", np.ones((1, 512), dtype=BF))
    add("ones_r32", np.ones((1, 512), dtype=np.float32))
    add("ones_c16", np.ones((128, 1), dtype=BF))
    add("ones_c32", np.ones((128, 1), dtype=np.float32))
    return c


def xn_view(xn_fm, cc, t0, t1):
    return xn_fm[:, cc * T + t0: cc * T + t1]


def emit_ln(b, c, x_dram, xn_fm, xn_tm=None, pb=None, bufs=None, xn32_d=None):
    if bufs is None:
        xt = [b.sb(f"ln_x{i}", [128, D], F32) for i in range(2)]
        sq = b.sb("ln_sq", [128, D], BF16)
        xnt = [b.sb(f"ln_xnt{i}", [128, D], BF16) for i in range(2)]
    else:
        xt = [bufs[0], bufs[1]]
        xnt = [bufs[2], bufs[3]]
        sq = bufs[4]
    stat = b.sb("ln_stat", [128, 16], F32)
    s_x = [b.sem("ln_ldx0"), b.sem("ln_ldx1")]
    s_sq = b.sem("ln_sq")
    s_sm = b.sem("ln_sm")
    s_sd = b.sem("ln_sd")
    s_xn = b.sem("ln_xn")
    s_tr = b.sem("ln_tr")
    s_ev = b.sem("ln_ev")
    s_x32 = b.sem("ln_x32") if xn32_d is not None else None
    xn_tok, sd_tok, tr_tok, x32_tok = [], [], [], []

    for i in range(NT):
        bi = i % 2
        xti = xt[bi]
        xnti = xnt[bi]
        a = stat[:, 8 * bi: 8 * bi + 8]
        if i >= 2:
            b.wait("sync", s_xn, xn_tok[i - 2])
            if xn32_d is not None:
                b.wait("sync", s_x32, x32_tok[i - 2])
        b.dma("sync", xti, x_dram[i * 128:(i + 1) * 128, :], s_x[bi])
        b.wait("scalar", s_x[bi])
        b.op("scalar", lambda e, o=sq, x_=xti: e.activation(
            o, x_, AF.Square), s_sq)
        b.wait("vector", s_x[bi])
        if i >= 2:
            b.wait("vector", s_sd, sd_tok[i - 2])
            b.wait("vector", s_tr, tr_tok[i - 2])
        b.op("vector", lambda e, x_=xti, a=a: e.reduce_sum(
            a[:, 1:2], x_, AX.X), s_sm)
        b.wait("vector", s_sq)
        b.op("vector", lambda e, a=a, s=sq: e.reduce_sum(
            a[:, 0:1], s, AX.X), s_sm)
        b.wait("vector", s_sm)
        b.op("vector", lambda e, a=a: e.tensor_scalar(
            a[:, 2:3], a[:, 1:2], 1.0 / D, None, ALU.mult), s_sm)
        b.op("vector", lambda e, a=a: e.tensor_scalar(
            a[:, 3:4], a[:, 0:1], 1.0 / D, 1e-5, ALU.mult, ALU.add), s_sm)
        b.wait("vector", s_sm)
        b.op("vector", lambda e, a=a: e.tensor_tensor(
            a[:, 4:5], a[:, 2:3], a[:, 2:3], ALU.mult), s_sm)
        b.wait("vector", s_sm)
        b.op("vector", lambda e, a=a: e.tensor_tensor(
            a[:, 3:4], a[:, 3:4], a[:, 4:5], ALU.subtract), s_sm)
        b.wait("scalar", s_sm)
        b.op("scalar", lambda e, a=a: e.activation(
            a[:, 4:5], a[:, 3:4], AF.Sqrt), s_sd)
        sd_tok.append(s_sd.n)
        b.wait("vector", s_sd)
        b.op("vector", lambda e, a=a: e.reciprocal(a[:, 5:6], a[:, 4:5]), s_sm)
        b.wait("vector", s_sm)
        b.op("vector", lambda e, a=a: e.tensor_tensor(
            a[:, 6:7], a[:, 2:3], a[:, 5:6], ALU.mult), s_sm)
        b.wait("vector", s_sm)
        b.op("vector", lambda e, a=a: e.tensor_scalar(
            a[:, 7:8], a[:, 6:7], -1.0, None, ALU.mult), s_sm)
        b.wait("vector", s_sm)
        if xn32_d is not None:
            b.op("vector", lambda e, o=xti, x_=xti, a=a: e.tensor_scalar(
                o, x_, a[:, 5:6], a[:, 7:8], ALU.mult, ALU.add), s_xn)
            b.wait("vector", s_xn)
            b.wait("sync", s_xn)
            b.op("vector", lambda e, o=xnti, x_=xti: e.tensor_copy(o, x_), s_xn)
            b.dma("sync", xn32_d[i * 128:(i + 1) * 128, :], xti, s_x32)
            x32_tok.append(s_x32.n)
        else:
            b.op("vector", lambda e, o=xnti, x_=xti, a=a: e.tensor_scalar(
                o, x_, a[:, 5:6], a[:, 7:8], ALU.mult, ALU.add), s_xn)
            x32_tok.append(0)
        if xn_tm is not None:
            b.wait("vector", s_xn)
            b.op("vector", lambda e, o=xn_tm[:, i * D:(i + 1) * D], x_=xnti:
                 e.tensor_copy(o, x_), s_xn)
        xn_tok.append(s_xn.n)
        if xn_fm is None:
            b.op("vector", lambda e: e.engine_nop(), s_tr)
            tr_tok.append(s_tr.n)
            b.op("vector", lambda e: e.engine_nop(), s_ev)
            continue
        pbank = pb[bi].bitcast(BF16)
        b.wait("tensor", s_xn)
        if i >= 2:
            b.wait("tensor", s_ev)
        for cc in range(8):
            b.op("tensor", lambda e, o=pbank[:, cc * 128:(cc + 1) * 128],
                 x_=xnti[:, cc * 128:(cc + 1) * 128], idt=c["ident"]:
                 e.transpose(o, x_, idt), s_tr if cc == 7 else None)
        tr_tok.append(s_tr.n)
        b.wait("vector", s_tr)
        dst = xn_fm.rearrange("p (c t) -> p c t", c=8)[:, :, i * 128:(i + 1) * 128]
        b.op("vector", lambda e, o=dst, s=pbank.rearrange("p (c t) -> p c t", c=8):
             e.tensor_copy(o, s), s_ev)
    return s_xn if xn_fm is None else s_ev


def emit_bcast_col(b, c, src11, dst_col, bank, s_mm, s_ev, scale=None):
    b.wait("tensor", s_ev)
    b.op("tensor", lambda e, o=bank[:, 0:1], w=c["ones_r32"][:, 0:128], r=src11:
         e.matmul(o, w, r, start=True, stop=True), s_mm)
    b.wait("vector", s_mm)
    if scale is None:
        b.op("vector", lambda e, o=dst_col, s=bank[:, 0:1]: e.tensor_copy(o, s), s_ev)
    else:
        b.op("vector", lambda e, o=dst_col, s=bank[:, 0:1]: e.tensor_scalar(
            o, s, scale, None, ALU.mult), s_ev)


def emit_out_accum(b, c, bank, xo_d, dc, tc, st32a, st32tr, trbank,
                   s_ev, s_tr, s_st, s_acc, s_mm_wait=None, bias_col=None,
                   scale_col=None, wait_acc_tok=None):
    b.wait("vector", s_tr)
    if s_mm_wait is not None:
        b.wait("vector", s_mm_wait[0], s_mm_wait[1])
    if bias_col is not None and scale_col is None:
        b.op("vector", lambda e, o=st32a, s=bank, bc=bias_col:
             e.tensor_scalar(o, s, bc, None, ALU.add), s_ev)
    elif scale_col is not None and bias_col is None:
        b.op("vector", lambda e, o=st32a, s=bank, sc=scale_col:
             e.tensor_scalar(o, s, sc, None, ALU.mult), s_ev)
    elif scale_col is not None and bias_col is not None:
        b.op("vector", lambda e, o=st32a, s=bank, sc=scale_col, bc=bias_col:
             e.tensor_scalar(o, s, sc, bc, ALU.mult, ALU.add), s_ev)
    else:
        b.op("vector", lambda e, o=st32a, s=bank: e.tensor_copy(o, s), s_ev)
    b.wait("tensor", s_ev)
    b.wait("tensor", s_st)
    for j in range(4):
        b.op("tensor", lambda e, o=trbank[:, j * 128:(j + 1) * 128],
             s=st32a[:, j * 128:(j + 1) * 128], idt=c["ident32"]:
             e.transpose(o, s, idt), s_tr if j == 3 else None)
    b.wait("vector", s_tr)
    b.wait("vector", s_acc)
    b.op("vector", lambda e, o=st32tr, s=trbank[:, 0:512]: e.tensor_copy(o, s), s_st)
    b.wait("gpsimd", s_st)
    if wait_acc_tok is not None:
        b.wait("gpsimd", wait_acc_tok[0], wait_acc_tok[1])
    dst = xo_d[tc * TC:(tc + 1) * TC, dc * 128:(dc + 1) * 128]
    b.dma("gpsimd", dst.rearrange("(j q) d -> q j d", j=4),
          st32tr.rearrange("q (j d) -> q j d", j=4), s_acc, accum_op=ALU.add)



def build_l0(parts=("attn", "ffn"), taps=(), b=None, x_d=None, xo_d=None, pfx=""):
    if b is None:
        b = B()
    nc = b.nc
    if x_d is None:
        x_d = nc.dram_tensor("x", [T, D], F32, kind="ExternalInput").ap()
    wqT = nc.dram_tensor(pfx + "wqT", [D, 128], F32, kind="ExternalInput").ap()
    wkT = nc.dram_tensor(pfx + "wkT", [D, 128], F32, kind="ExternalInput").ap()
    wvT = nc.dram_tensor(pfx + "wvT", [D, 128], F32, kind="ExternalInput").ap()
    bq_d = nc.dram_tensor(pfx + "bq", [1, 128], F32, kind="ExternalInput").ap()
    bk_d = nc.dram_tensor(pfx + "bk", [1, 128], F32, kind="ExternalInput").ap()
    bv_d = nc.dram_tensor(pfx + "bv", [1, 128], F32, kind="ExternalInput").ap()
    woT = nc.dram_tensor(pfx + "woT", [128, D], F32, kind="ExternalInput").ap()
    bo_d = nc.dram_tensor(pfx + "bo", [128, 8], F32, kind="ExternalInput").ap()
    k0_d = nc.dram_tensor(pfx + "k0", [2 * D, 4096], U8, kind="ExternalInput").ap()
    k1_d = nc.dram_tensor(pfx + "k1", [4096, 4096], U8, kind="ExternalInput").ap()
    k2_d = nc.dram_tensor(pfx + "k2", [4096, D], U8, kind="ExternalInput").ap()
    s_d = nc.dram_tensor(pfx + "sces", [1, 3], F32, kind="ExternalInput").ap()
    if xo_d is None:
        xo_d = nc.dram_tensor("xo", [T, D], F32, kind="ExternalOutput").ap()

    c = load_consts(b)
    mask_np = np.zeros((4, 128, 512), dtype=BF)
    for r in range(4):
        for kk in range(128):
            q0 = r * 128 + kk
            if q0 < 512:
                mask_np[r, kk, q0:] = 1.0
    mask_d = nc.inline_tensor(mask_np.reshape(4 * 128, 512), name="c_l0mask").ap()
    masks = b.sb("s_l0mask", [128, 4 * 512], BF16)
    for r in range(4):
        b.dma("sync", masks[:, r * 512:(r + 1) * 512],
              mask_d[r * 128:(r + 1) * 128, :], c["ld"])

    xn_fm = b.sb("xn_fm", [128, 8 * T], BF16)
    wk16 = b.sb("wk16", [128, 52 * 1024], BF16)
    wk32 = b.sb("wk32", [128, 2496], F32)
    pb = [b.ps(f"pb{i}", [128, 512], F32) for i in range(8)]

    s_mm = b.sem("mm")
    s_ev = b.sem("ev")
    s_init = b.sem("init")
    s_acc = b.sem("acc")

    b.dma("gpsimd", xo_d, x_d, s_init)

    ln_bufs = (wk16[:, 46080:48128].bitcast(F32), wk16[:, 48128:50176].bitcast(F32),
               wk16[:, 50176:51200], wk16[:, 51200:52224], wk16[:, 52224:53248])
    b.wait("sync", c["ld"])
    sA = emit_ln(b, c, x_d, xn_fm, pb=[pb[6], pb[7]], bufs=ln_bufs)

    wq_s = b.sb("wq_s", [128, 8 * 128], BF16)
    wk_s = b.sb("wk_s", [128, 8 * 128], BF16)
    wv_s = b.sb("wv_s", [128, 8 * 128], BF16)
    wo_s = b.sb("wo_s", [128, D], BF16)
    bq_s = b.sb("bq_s", [1, 128], BF16)
    bk_s = b.sb("bk_s", [1, 128], BF16)
    bv_s = b.sb("bv_s", [1, 128], BF16)
    bo_s = b.sb("bo_s", [128, 8], F32)
    s_wld = b.sem("wld")
    b.dma("gpsimd", wq_s.rearrange("p (c h) -> p c h", c=8),
          wqT.rearrange("(c p) h -> p c h", p=128), s_wld)
    b.dma("gpsimd", wk_s.rearrange("p (c h) -> p c h", c=8),
          wkT.rearrange("(c p) h -> p c h", p=128), s_wld)
    b.dma("gpsimd", wv_s.rearrange("p (c h) -> p c h", c=8),
          wvT.rearrange("(c p) h -> p c h", p=128), s_wld)
    b.dma("gpsimd", wo_s, woT, s_wld)
    b.dma("gpsimd", bq_s, bq_d, s_wld)
    b.dma("gpsimd", bk_s, bk_d, s_wld)
    b.dma("gpsimd", bv_s, bv_d, s_wld)
    b.dma("gpsimd", bo_s, bo_d, s_wld)

    q_fm = wk16[:, 0:T]
    k_fm = wk16[:, T:2 * T]
    v_tm = wk16[:, 2 * T:3 * T]
    p_buf = wk16[:, 3 * T:7 * T]
    o_fm = wk16[:, 7 * T:8 * T]
    st16a = wk16[:, 8 * T:8 * T + 512]
    st32a = wk32[:, 0:512]
    st32tr = wk32[:, 512:1024]
    recb = wk32[:, 1024:1536]
    bias_m = wk32[:, 1536:1537]
    mstat = wk32[:, 1537:1541]
    rrow = wk32[0:1, 1792:2304]

    for e in ("tensor", "scalar"):
        b.wait(e, sA)
    b.wait("tensor", s_wld)
    b.wait("tensor", c["ld"])

    SCALE = 128.0 ** -0.5
    ev_tok = {}
    for (w_s, bias_r, dst, tag) in ((wq_s, bq_s, q_fm, "q"), (wk_s, bk_s, k_fm, "k")):
        for tc in range(NTC):
            bank = pb[tc % 2]
            if tc >= 2:
                b.wait("tensor", s_ev, ev_tok[(tag, tc - 2)])
            for cc in range(8):
                b.op("tensor", lambda e, o=bank, w=w_s[:, cc * 128:(cc + 1) * 128],
                     r=xn_view(xn_fm, cc, tc * TC, (tc + 1) * TC), st=(cc == 0):
                     e.matmul(o, w, r, start=st, stop=False))
            b.op("tensor", lambda e, o=bank, w=bias_r, r=c["ones_r16"]:
                 e.matmul(o, w, r, start=False, stop=True), s_mm)
            b.wait("vector", s_mm)
            if tag == "q":
                b.op("vector", lambda e, o=dst[:, tc * TC:(tc + 1) * TC], s=bank:
                     e.tensor_scalar(o, s, SCALE, None, ALU.mult), s_ev)
            else:
                b.op("vector", lambda e, o=dst[:, tc * TC:(tc + 1) * TC], s=bank:
                     e.tensor_copy(o, s), s_ev)
            ev_tok[(tag, tc)] = s_ev.n

    for ti in range(NT):
        bank = pb[2 + ti % 2]
        if ti >= 2:
            b.wait("tensor", s_ev, ev_tok[("v", ti - 2)])
        for cc in range(8):
            b.op("tensor", lambda e, o=bank[0:128, 0:128],
                 w=xn_view(xn_fm, cc, ti * 128, (ti + 1) * 128),
                 r=wv_s[:, cc * 128:(cc + 1) * 128], st=(cc == 0):
                 e.matmul(o, w, r, start=st, stop=False))
        b.op("tensor", lambda e, o=bank[0:128, 0:128], w=c["ones_r16"][:, 0:128],
             r=bv_s: e.matmul(o, w, r, start=False, stop=True), s_mm)
        b.wait("vector", s_mm)
        b.op("vector", lambda e, o=v_tm[:, ti * 128:(ti + 1) * 128],
             s=bank[:, 0:128]: e.tensor_copy(o, s), s_ev)
        ev_tok[("v", ti)] = s_ev.n

    s_nrm = b.sem("nrm")
    for (src, cell) in ((q_fm, 0), (k_fm, 1)):
        for t2 in range(NTC):
            b.wait("scalar", s_ev)
            b.wait("scalar", s_mm)
            b.op("scalar", lambda e, o=st32a, s=src[:, t2 * TC:(t2 + 1) * TC]:
                 e.activation(o, s, AF.Square), s_nrm)
            b.wait("tensor", s_nrm)
            b.op("tensor", lambda e, o=pb[4][0:1, :], w=c["ones_c32"], r=st32a:
                 e.matmul(o, w, r, start=True, stop=True), s_mm)
            b.wait("vector", s_mm)
            if t2 == 0:
                b.op("vector", lambda e, o=mstat[0:1, cell:cell + 1], s=pb[4][0:1, :]:
                     e.reduce_max(o, s, AX.X), s_nrm)
            else:
                b.op("vector", lambda e, o=mstat[0:1, 2:3], s=pb[4][0:1, :]:
                     e.reduce_max(o, s, AX.X), s_nrm)
                b.wait("vector", s_nrm)
                b.op("vector", lambda e, o=mstat[0:1, cell:cell + 1],
                     a=mstat[0:1, cell:cell + 1], bb=mstat[0:1, 2:3]:
                     e.tensor_tensor(o, a, bb, ALU.max), s_nrm)
    b.wait("scalar", s_nrm)
    b.op("scalar", lambda e, o=mstat[0:1, 0:1], s=mstat[0:1, 0:1]:
         e.activation(o, s, AF.Sqrt), s_nrm)
    b.op("scalar", lambda e, o=mstat[0:1, 1:2], s=mstat[0:1, 1:2]:
         e.activation(o, s, AF.Sqrt), s_nrm)
    b.wait("vector", s_nrm)
    b.op("vector", lambda e, o=mstat[0:1, 0:1], a=mstat[0:1, 0:1], bb=mstat[0:1, 1:2]:
         e.tensor_tensor(o, a, bb, ALU.mult), s_nrm)
    b.wait("vector", s_nrm)
    b.op("vector", lambda e, o=mstat[0:1, 0:1], a=mstat[0:1, 0:1]:
         e.tensor_scalar(o, a, -1.0, -1.0, ALU.mult, ALU.add), s_nrm)
    b.wait("tensor", s_nrm)
    emit_bcast_col(b, c, mstat[0:1, 0:1], bias_m, pb[4], s_mm, s_ev)

    s_p = b.sem("p")
    s_msk = b.sem("msk")
    s_sum = b.sem("sum")
    s_pv = b.sem("pv")
    s_ob = b.sem("ob")
    exp_tok = {}
    b.wait("scalar", s_ev)
    for qc in range(NTC):
        nk = (qc + 1) * 4
        for kt in range(nk):
            bank = pb[kt % 2]
            key = (qc, kt - 2) if kt >= 2 else (qc - 1, None)
            if kt >= 2:
                b.wait("tensor", s_p, exp_tok[(qc, kt - 2)])
            b.op("tensor", lambda e, o=bank, w=k_fm[:, kt * 128:(kt + 1) * 128],
                 r=q_fm[:, qc * TC:(qc + 1) * TC]:
                 e.matmul(o, w, r, start=True, stop=True), s_mm)
            b.wait("scalar", s_mm)
            dst = p_buf[:, kt * 512:(kt + 1) * 512]
            b.op("scalar", lambda e, o=dst, s=bank, bm=bias_m:
                 e.activation(o, s, AF.Exp, bias=bm), s_p)
            exp_tok[(qc, kt)] = s_p.n
            if kt >= nk - 4:
                r = kt - (nk - 4)
                b.wait("vector", s_p)
                b.op("vector", lambda e, o=dst, a=dst,
                     m=masks[:, r * 512:(r + 1) * 512]:
                     e.tensor_tensor(o, a, m, ALU.mult), s_msk)
        b.wait("tensor", s_p)
        b.wait("tensor", s_msk)
        b.wait("tensor", s_ev)
        b.wait("tensor", s_ob)
        for kt in range(nk):
            b.op("tensor", lambda e, o=pb[2][0:1, :], w=c["ones_c16"],
                 r=p_buf[:, kt * 512:(kt + 1) * 512],
                 st=(kt == 0), sp=(kt == nk - 1):
                 e.matmul(o, w, r, start=st, stop=sp),
                 s_sum if kt == nk - 1 else None)
        for kt in range(nk):
            b.op("tensor", lambda e, o=pb[3], w=v_tm[:, kt * 128:(kt + 1) * 128],
                 r=p_buf[:, kt * 512:(kt + 1) * 512],
                 st=(kt == 0), sp=(kt == nk - 1):
                 e.matmul(o, w, r, start=st, stop=sp),
                 s_pv if kt == nk - 1 else None)
        b.wait("vector", s_sum)
        b.op("vector", lambda e, o=rrow[:, 0:512], s=pb[2][0:1, :]:
             e.reciprocal(o, s), s_ev)
        b.wait("tensor", s_ev)
        b.op("tensor", lambda e, o=pb[4], w=c["ones_r32"][:, 0:128], r=rrow[:, 0:512]:
             e.matmul(o, w, r, start=True, stop=True), s_mm)
        b.wait("vector", s_mm)
        b.op("vector", lambda e, o=recb, s=pb[4]: e.tensor_copy(o, s), s_ev)
        b.wait("vector", s_pv)
        b.wait("vector", s_ev)
        b.op("vector", lambda e, o=o_fm[:, qc * TC:(qc + 1) * TC], s=pb[3], r=recb:
             e.tensor_tensor(o, s, r, ALU.mult), s_ob)

    s_tr = b.sem("tr")
    s_st = b.sem("st")
    b.wait("tensor", s_ob)
    b.wait("gpsimd", s_init)
    for dc in range(8):
        for tc in range(NTC):
            bank = pb[tc % 2]
            b.wait("tensor", s_ev)
            b.op("tensor", lambda e, o=bank, w=wo_s[:, dc * 128:(dc + 1) * 128],
                 r=o_fm[:, tc * TC:(tc + 1) * TC]:
                 e.matmul(o, w, r, start=True, stop=True), s_mm)
            emit_out_accum(b, c, bank, xo_d, dc, tc, st32a, st32tr, pb[5],
                           s_ev, s_tr, s_st, s_acc, s_mm_wait=(s_mm, s_mm.n),
                           bias_col=bo_s[:, dc:dc + 1])
    attn_acc_total = s_acc.n

    if "ffn" not in parts:
        tap_map = {"xn_fm": xn_fm, "q_fm": q_fm, "k_fm": k_fm, "v_tm": v_tm,
                   "o_fm": o_fm, "p_buf": p_buf, "bias_m": bias_m,
                   "mstat": mstat, "wk32": wk32}
        for e_ in ("gpsimd",):
            for sm in (s_ev, s_ob, s_acc, s_mm, sA):
                b.wait(e_, sm)
        s_tap = b.sem("tap")
        for name in taps:
            ap = tap_map[name]
            dt_np = {BF16: "bf", F32: "f32"}
            out_d = nc.dram_tensor("tap_" + name, list(ap.shape),
                                   ap.dtype, kind="ExternalOutput").ap()
            b.dma("gpsimd", out_d, ap, s_tap)
        b.wait("gpsimd", s_tap)
        b.wait("gpsimd", s_acc)
        return b, x_d
    h1r = wk16[:, 0:32 * 512]
    h2r = wk16[:, 32 * 512:64 * 512]
    kp = wk16[:, 64 * 512:96 * 512]
    xneg = wk16[:, 96 * 512:104 * 512]
    srow = wk32[0:1, 1024:2048]
    scol = b.sb("scol", [128, 4], F32)
    s_s = b.sem("scales")
    sces_s = b.sb("sces_s", [1, 3], F32)
    b.dma("sync", sces_s, s_d, s_s)
    b.wait("tensor", s_s)
    for i, fac in enumerate((4.0, 4.0, 2.0)):
        emit_bcast_col(b, c, sces_s[0:1, i:i + 1], scol[:, i:i + 1], pb[4],
                       s_mm, s_ev, scale=fac)

    s_h = b.sem("h")
    s_xng = b.sem("xng")
    kslot = [kp[:, s * 4096:(s + 1) * 4096] for s in range(4)]
    s_kp = [b.sem(f"kp{s}") for s in range(4)]

    ptab = []
    for _tc in range(NTC):
        ptab += [(k0_d, 16, f * 128) for f in range(32)]
        ptab += [(k1_d, 32, f * 128) for f in range(32)]
        ptab += [(k2_d, 32, f * 128) for f in range(8)]
    dma_tok = {}
    mm_tok = {}

    def kp_issue(q):
        if q >= len(ptab):
            return
        k_d, nci, f0 = ptab[q]
        s = q % 4
        if q >= 4:
            b.wait("gpsimd", s_mm, mm_tok[q - 4])
        b.dma("gpsimd",
              kslot[s][:, 0:nci * 128].rearrange("p (ci f) -> p ci f", f=128),
              k_d[:, f0:f0 + 128].rearrange("(ci p) f -> p ci f", p=128),
              s_kp[s])
        dma_tok[q] = s_kp[s].n

    def kp_view(q):
        nci = ptab[q][1]
        return kslot[q % 4][:, 0:nci * 128].rearrange("p (ci f) -> p ci f", f=128)

    h_tok2 = [0, 0]
    b.wait("gpsimd", sA)
    for q in range(L0_PREFETCH):
        kp_issue(q)
    qi = 0

    for tc in range(NTC):
        if tc == 0:
            b.wait("vector", s_ob)
        else:
            b.wait("vector", s_h)
        src3 = xn_fm.rearrange("p (c t) -> p c t", c=8)[:, :, tc * TC:(tc + 1) * TC]
        b.op("vector", lambda e, o=xneg.rearrange("p (c t) -> p c t", c=8), s=src3:
             e.tensor_scalar(o, s, -1.0, None, ALU.mult), s_xng)

        for fc in range(32):
            q = qi; qi += 1
            kp_issue(q + L0_PREFETCH)
            par = q % 2
            bank = pb[par]
            b.wait("tensor", s_h, h_tok2[par])
            b.wait("tensor", s_kp[q % 4], dma_tok[q])
            if fc == 0:
                b.wait("tensor", s_xng)
            kv = kp_view(q)
            for ci in range(16):
                r_ = (xn_view(xn_fm, ci, tc * TC, (tc + 1) * TC) if ci < 8
                      else xneg[:, (ci - 8) * 512:(ci - 7) * 512])
                b.op("tensor", lambda e, o=bank, w=kv[:, ci], r=r_,
                     st=(ci == 0), sp=(ci == 15):
                     e.matmul(o, w, r, start=st, stop=sp),
                     s_mm if ci == 15 else None)
            mm_tok[q] = s_mm.n
            b.wait("scalar", s_mm)
            b.op("scalar", lambda e, o=h1r[:, fc * 512:(fc + 1) * 512], s=bank,
                 sc=scol[:, 0:1]: e.activation(o, s, AF.Relu, scale=sc), s_h)
            h_tok2[par] = s_h.n

        b.wait("tensor", s_h)
        b.wait("tensor", s_ev)
        for fc in range(32):
            b.op("tensor", lambda e, o=pb[2][0:1, :], w=c["ones_c16"],
                 r=h1r[:, fc * 512:(fc + 1) * 512],
                 st=(fc == 0), sp=(fc == 31):
                 e.matmul(o, w, r, start=st, stop=sp),
                 s_sum if fc == 31 else None)
        b.wait("vector", s_sum)
        b.op("vector", lambda e, o=srow[0:1, 0:512], s=pb[2][0:1, :]:
             e.tensor_scalar(o, s, -0.5, None, ALU.mult), s_ev)
        s1_tok = s_ev.n

        for fc in range(32):
            q = qi; qi += 1
            kp_issue(q + L0_PREFETCH)
            par = q % 2
            bank = pb[par]
            b.wait("tensor", s_h, h_tok2[par])
            b.wait("tensor", s_kp[q % 4], dma_tok[q])
            if fc == 0:
                b.wait("tensor", s_ev, s1_tok)
            kv = kp_view(q)
            for ci in range(32):
                b.op("tensor", lambda e, o=bank, w=kv[:, ci],
                     r=h1r[:, ci * 512:(ci + 1) * 512], st=(ci == 0):
                     e.matmul(o, w, r, start=st, stop=False))
            b.op("tensor", lambda e, o=bank, w=c["ones_r32"][:, 0:128],
                 r=srow[0:1, 0:512]:
                 e.matmul(o, w, r, start=False, stop=True), s_mm)
            mm_tok[q] = s_mm.n
            b.wait("scalar", s_mm)
            b.op("scalar", lambda e, o=h2r[:, fc * 512:(fc + 1) * 512], s=bank,
                 sc=scol[:, 1:2]: e.activation(o, s, AF.Relu, scale=sc), s_h)
            h_tok2[par] = s_h.n

        b.wait("tensor", s_h)
        b.wait("tensor", s_ev)
        for fc in range(32):
            b.op("tensor", lambda e, o=pb[2][0:1, :], w=c["ones_c16"],
                 r=h2r[:, fc * 512:(fc + 1) * 512],
                 st=(fc == 0), sp=(fc == 31):
                 e.matmul(o, w, r, start=st, stop=sp),
                 s_sum if fc == 31 else None)
        b.wait("vector", s_sum)
        b.op("vector", lambda e, o=srow[0:1, 512:1024], s=pb[2][0:1, :]:
             e.tensor_scalar(o, s, -0.5, None, ALU.mult), s_ev)
        s2_tok = s_ev.n

        for dc in range(8):
            q = qi; qi += 1
            kp_issue(q + L0_PREFETCH)
            par = q % 2
            bank = pb[2 + par]
            b.wait("tensor", s_kp[q % 4], dma_tok[q])
            if dc == 0:
                b.wait("tensor", s_ev, s2_tok)
            b.wait("tensor", s_ev)
            kv = kp_view(q)
            for ci in range(32):
                b.op("tensor", lambda e, o=bank, w=kv[:, ci],
                     r=h2r[:, ci * 512:(ci + 1) * 512], st=(ci == 0):
                     e.matmul(o, w, r, start=st, stop=False))
            b.op("tensor", lambda e, o=bank, w=c["ones_r32"][:, 0:128],
                 r=srow[0:1, 512:1024]:
                 e.matmul(o, w, r, start=False, stop=True), s_mm)
            mm_tok[q] = s_mm.n
            emit_out_accum(
                b, c, bank, xo_d, dc, tc,
                st32a, st32tr, pb[5], s_ev, s_tr, s_st, s_acc,
                s_mm_wait=(s_mm, s_mm.n), scale_col=scol[:, 2:3],
                wait_acc_tok=((s_acc, attn_acc_total)
                              if (tc == 0 and dc == 0) else None))

    b.wait("gpsimd", s_acc)
    return b, x_d


def host_prep_l0(inputs):
    def t(a):
        return np.ascontiguousarray(np.asarray(a, dtype=np.float32).T)

    def u8(a):
        a = np.asarray(a)
        return a.view(np.uint8) if a.dtype == np.bool_ else a.astype(np.uint8)

    return {
        "wqT": t(inputs["L0_attn_Wq"]), "wkT": t(inputs["L0_attn_Wk"]),
        "wvT": t(inputs["L0_attn_Wv"]),
        "bq": np.asarray(inputs["L0_attn_bq"], np.float32).reshape(1, 128),
        "bk": np.asarray(inputs["L0_attn_bk"], np.float32).reshape(1, 128),
        "bv": np.asarray(inputs["L0_attn_bv"], np.float32).reshape(1, 128),
        "woT": t(inputs["L0_attn_Wo"]),
        "bo": np.ascontiguousarray(np.asarray(inputs["L0_attn_bo"], np.float32).reshape(8, 128).T),
        "k0": u8(inputs["L0_ffn_k0"]), "k1": u8(inputs["L0_ffn_k1"]),
        "k2": u8(inputs["L0_ffn_k2"]),
        "sces": np.array([[inputs["L0_ffn_s0"], inputs["L0_ffn_s1"],
                           inputs["L0_ffn_s2"]]], dtype=np.float32),
    }


def run_l0(x_full, inputs):
    b, _ = build_l0()
    nc = b.finish()
    w = host_prep_l0(inputs)
    in_maps = []
    for core in range(8):
        m = dict(w)
        m["x"] = np.ascontiguousarray(x_full[core], dtype=np.float32)
        in_maps.append(m)
    res = run_bass_kernel_spmd(nc, in_maps, core_ids=list(range(8)))
    return np.stack([res.results[i]["xo"] for i in range(8)], axis=0)






QN = 12582912.0
L0_PREFETCH = 2


def emit_quant(b, src, dst, qs, s_q, s_qa, nb):
    N = nb * 16
    ax = qs["ax"][:, 0:N]
    sg = qs["sg"][:, 0:N]
    cc_ = qs["cc"][:, 0:N]
    ct = qs["ct"][:, 0:N]
    am = qs["am"][:, 0:nb]
    sc_ = qs["sc"][:, 0:nb]
    b.op("scalar", lambda e, o=ax, s=src: e.activation(o, s, AF.Abs), s_qa)
    b.op("scalar", lambda e, o=sg, s=src: e.activation(o, s, AF.Sign), s_qa)
    b.wait("vector", s_qa)
    b.op("vector", lambda e, o=am, s=ax.rearrange("p (n k) -> p n k", k=16):
         e.reduce_max(o, s, AX.X), s_q)
    b.wait("vector", s_q)
    b.op("vector", lambda e, o=sc_, s=am: e.tensor_scalar(
        o, s, 1.0 / 6.0, 1e-12, ALU.mult, ALU.max), s_q)
    b.wait("vector", s_q)
    b.op("vector", lambda e, o=am, s=sc_: e.reciprocal(o, s), s_q)
    b.wait("vector", s_q)
    b.op("vector", lambda e, o=ax.rearrange("p (n k) -> p n k", k=16),
         a=ax.rearrange("p (n k) -> p n k", k=16),
         r=am.unsqueeze(2).broadcast_to([128, nb, 16]):
         e.tensor_tensor(o, a, r, ALU.mult), s_q)
    b.wait("vector", s_q)
    b.op("vector", lambda e, o=cc_, s=ax: e.tensor_scalar(o, s, 2.5, None, ALU.is_ge), s_q)
    b.op("vector", lambda e, o=ct, s=ax: e.tensor_scalar(o, s, 3.5, None, ALU.is_ge), s_q)
    b.wait("vector", s_q)
    b.op("vector", lambda e, o=cc_, a=cc_, bb=ct: e.tensor_tensor(o, a, bb, ALU.add), s_q)
    b.op("vector", lambda e, o=ct, s=ax: e.tensor_scalar(
        o, s, 5.0, 2.0, ALU.is_ge, ALU.mult), s_q)
    b.wait("vector", s_q)
    b.op("vector", lambda e, o=cc_, a=cc_, bb=ct: e.tensor_tensor(o, a, bb, ALU.add), s_q)
    b.op("vector", lambda e, o=ax, s=ax: e.tensor_scalar(o, s, 2.0, None, ALU.min), s_q)
    b.wait("vector", s_q)
    b.op("vector", lambda e, o=ax, s=ax: e.tensor_scalar(
        o, s, 2.0, QN, ALU.mult, ALU.add), s_q)
    b.wait("vector", s_q)
    b.op("vector", lambda e, o=ax, s=ax: e.tensor_scalar(
        o, s, -QN, 0.5, ALU.add, ALU.mult), s_q)
    b.wait("vector", s_q)
    b.op("vector", lambda e, o=ax, a=ax, bb=cc_: e.tensor_tensor(o, a, bb, ALU.add), s_q)
    b.wait("vector", s_q)
    b.op("vector", lambda e, o=ax, a=ax, bb=sg: e.tensor_tensor(o, a, bb, ALU.mult), s_q)
    b.wait("vector", s_q)
    b.op("vector", lambda e, o=dst.rearrange("p (n k) -> p n k", k=16),
         a=ax.rearrange("p (n k) -> p n k", k=16),
         r=sc_.unsqueeze(2).broadcast_to([128, nb, 16]):
         e.tensor_tensor(o, a, r, ALU.mult), s_q)


class Cur:

    def __init__(self, region):
        self.r = region
        self.n = region.shape[1]
        self.i = 0

    def take(self, cols, f32=False):
        cols2 = cols * (2 if f32 else 1)
        assert self.i + cols2 <= self.n, f"region overflow {self.i}+{cols2}>{self.n}"
        v = self.r[:, self.i:self.i + cols2]
        self.i += cols2
        return v.bitcast(F32) if f32 else v


def build_l3(parts=("attn", "ffn"), taps=(), b=None, x_d=None, xo_d=None, pfx=""):
    if b is None:
        b = B()
    nc = b.nc
    DH = 64
    if x_d is None:
        x_d = nc.dram_tensor("x", [T, D], F32, kind="ExternalInput").ap()
    qkvT = nc.dram_tensor(pfx + "qkvT", [D, 3 * D], F32, kind="ExternalInput").ap()
    qkvb_d = nc.dram_tensor(pfx + "qkvb", [1, 3072], F32, kind="ExternalInput").ap()
    projT = nc.dram_tensor(pfx + "projT", [D, D], F32, kind="ExternalInput").ap()
    projb = nc.dram_tensor(pfx + "projb", [128, 8], F32, kind="ExternalInput").ap()
    wg_d = nc.dram_tensor(pfx + "wg", [2816, D], F32, kind="ExternalInput").ap()
    wu_d = nc.dram_tensor(pfx + "wu", [2816, D], F32, kind="ExternalInput").ap()
    wd_d = nc.dram_tensor(pfx + "wd", [D, 2816], F32, kind="ExternalInput").ap()
    if xo_d is None:
        xo_d = nc.dram_tensor("xo", [T, D], F32, kind="ExternalOutput").ap()
    wgT_s = nc.dram_tensor("wgT_s", [D, 2816], BF16, kind="Internal").ap()
    wuT_s = nc.dram_tensor("wuT_s", [D, 2816], BF16, kind="Internal").ap()
    wdT_s = nc.dram_tensor("wdT_s", [2816, D], BF16, kind="Internal").ap()
    xn32_s = nc.dram_tensor("l3xn32", [T, D], F32, kind="Internal").ap()

    c = load_consts(b)
    pos = np.arange(T, dtype=np.float64)
    inv = 10000.0 ** (-(np.arange(0, DH, 2, dtype=np.float64)) / DH)
    fr = (pos[:, None] * inv[None, :])
    SCL = DH ** -0.25
    rope_s = {}
    for nm, arr in (("cos", np.cos(fr) * SCL), ("sin", np.sin(fr) * SCL)):
        a32 = np.ascontiguousarray(arr.astype(BF).reshape(32, 128, 32))
        d_ = nc.inline_tensor(a32, name="c_l3" + nm).ap()
        s_ = b.sb("sc_l3" + nm, [128, 32 * 32], BF16)
        b.dma("sync", s_.rearrange("p (t k) -> p t k", t=32),
              d_.rearrange("t p k -> p t k"), c["ld"])
        rope_s[nm] = s_
    rope_s["cosq"] = rope_s["cosk"] = rope_s["cos"]
    rope_s["sinq"] = rope_s["sink"] = rope_s["sin"]

    xn_fm = b.sb("xn_fm", [128, 8 * T], BF16)
    o_all = b.sb("o_all", [128, 8 * T], BF16)
    wka = b.sb("wka", [128, 29440], BF16)
    wk32 = b.sb("wk32", [128, 2496], F32)
    pb = [b.ps(f"pb{i}", [128, 512], F32) for i in range(8)]

    s_mm = b.sem("mm")
    s_ev = b.sem("ev")
    s_init = b.sem("init")
    s_acc = b.sem("acc")
    s_tr = b.sem("tr")
    s_st = b.sem("st")
    s_wld = b.sem("wld")
    s_rope = b.sem("rope")
    s_p = b.sem("p")
    s_pv = b.sem("pv")
    s_ob = b.sem("ob")

    b.dma("gpsimd", xo_d, x_d, s_init)

    ca = Cur(wka)
    q_hp = ca.take(T)
    k_hp = ca.take(T)
    v_hp = ca.take(32 * 130)
    p_buf = ca.take(8 * 1024)
    wq_p = p_buf[:, 0:3 * 1024]
    st16 = ca.take(128)
    qkvb_s = ca.take(3072, f32=True)[0:1, :]

    st32a = wk32[:, 0:512]
    st32tr = wk32[:, 512:1024]
    recb = wk32[:, 1024:1536]
    bias_m = wk32[:, 1536:1537]
    mstat = wk32[:, 1537:1541]
    rrow = wk32[0:1, 1792:2304]
    ttmp = wk32[:, 2304:2368]
    ttmp2 = wk32[:, 2368:2432]
    pjb_s = wk32[:, 2432:2440]

    ln_bufs = (q_hp[:, 0:2048].bitcast(F32), q_hp[:, 2048:4096].bitcast(F32),
               k_hp[:, 0:1024], k_hp[:, 1024:2048], k_hp[:, 2048:3072])
    b.wait("sync", c["ld"])
    sA = emit_ln(b, c, x_d, xn_fm, pb=[pb[6], pb[7]], bufs=ln_bufs,
                 xn32_d=xn32_s)

    for e_ in ("tensor", "scalar"):
        b.wait(e_, sA)
    b.wait("tensor", c["ld"])

    b.dma("gpsimd", pjb_s, projb, s_wld)

    vv = v_hp.rearrange("p (t k) -> p t k", k=130)
    b.op("vector", lambda e, o=vv[:, :, 64:65]: e.memset(o, 1.0), s_ev)
    b.op("vector", lambda e, o=vv[:, :, 129:130]: e.memset(o, 1.0), s_ev)
    ones_tok = s_ev.n
    b.wait("tensor", s_wld)

    wqp3 = wq_p.rearrange("p (j k) -> p j k", j=3)
    for pair in range(8):
        s_wp = b.sem(f"wp{pair}")
        if pair > 0:
            b.wait("gpsimd", s_ob)
        else:
            b.wait("gpsimd", sA)
        for j, col0 in enumerate((pair * 128, D + pair * 128, 2 * D + pair * 128)):
            b.dma("gpsimd",
                  wqp3[:, j].rearrange("p (cc k) -> p cc k", cc=8),
                  qkvT[:, col0:col0 + 128].rearrange("(cc p) k -> p cc k", p=128),
                  s_wp)
        b.dma("gpsimd", qkvb_s, qkvb_d, s_wp)
        b.wait("tensor", s_wp)
        if pair == 0:
            b.wait("tensor", s_ev, ones_tok)
        else:
            b.wait("tensor", s_ob)

        for ti in range(NT):
            for j in range(3):
                bank = pb[j]
                for cc2 in range(8):
                    b.op("tensor", lambda e, o=bank[:, 0:128],
                         w=xn_view(xn_fm, cc2, ti * 128, (ti + 1) * 128),
                         r=wqp3[:, j].rearrange("p (cc k) -> p cc k", cc=8)[:, cc2],
                         st=(cc2 == 0):
                         e.matmul(o, w, r, start=st, stop=False))
                b0 = j * 1024 + pair * 128
                b.op("tensor", lambda e, o=bank[:, 0:128], w=c["ones_r32"][:, 0:128],
                     r=qkvb_s[0:1, b0:b0 + 128]:
                     e.matmul(o, w, r, start=False, stop=True), s_mm)
            b.wait("vector", s_mm)
            for (j, cosn, sinn, dstf) in ((0, "cosq", "sinq", q_hp),
                                          (1, "cosk", "sink", k_hp)):
                bank = pb[j]
                if j == 1:
                    b.wait("vector", s_tr)
                bview = bank[:, 0:128].rearrange("p (h k) -> p h k", k=64)
                ev_ = bview[:, :, 0:32]
                ov_ = bview[:, :, 32:64]
                cosb = rope_s[cosn][:, ti * 32:(ti + 1) * 32].unsqueeze(1).broadcast_to([128, 2, 32])
                sinb = rope_s[sinn][:, ti * 32:(ti + 1) * 32].unsqueeze(1).broadcast_to([128, 2, 32])
                t1 = ttmp.rearrange("p (h k) -> p h k", k=32)[:, 0:2]
                t2 = ttmp2.rearrange("p (h k) -> p h k", k=32)[:, 0:2]
                dst = st16.rearrange("p (h k) -> p h k", k=64)
                b.op("vector", lambda e, o=t1, a=ev_, r=cosb:
                     e.tensor_tensor(o, a, r, ALU.mult), s_rope)
                b.op("vector", lambda e, o=t2, a=ov_, r=sinb:
                     e.tensor_tensor(o, a, r, ALU.mult), s_rope)
                b.wait("vector", s_rope)
                b.op("vector", lambda e, o=dst[:, :, 0:32], a=t1, bb=t2:
                     e.tensor_tensor(o, a, bb, ALU.subtract), s_rope)
                b.op("vector", lambda e, o=t1, a=ev_, r=sinb:
                     e.tensor_tensor(o, a, r, ALU.mult), s_rope)
                b.op("vector", lambda e, o=t2, a=ov_, r=cosb:
                     e.tensor_tensor(o, a, r, ALU.mult), s_rope)
                b.wait("vector", s_rope)
                b.op("vector", lambda e, o=dst[:, :, 32:64], a=t1, bb=t2:
                     e.tensor_tensor(o, a, bb, ALU.add), s_rope)
                b.wait("tensor", s_rope)
                b.wait("tensor", s_ev)
                b.op("tensor", lambda e, o=pb[3].bitcast(BF16)[:, 0:128],
                     s=st16, idt=c["ident"]:
                     e.transpose(o, s, idt), s_tr)
                b.wait("vector", s_tr)
                b.op("vector", lambda e, o=dstf[:, ti * 128:(ti + 1) * 128],
                     s=pb[3].bitcast(BF16)[:, 0:128]: e.tensor_copy(o, s), s_ev)
            b.op("vector", lambda e, o=vv[:, ti, 0:64], s=pb[2][:, 0:64]:
                 e.tensor_copy(o, s), s_ev)
            b.op("vector", lambda e, o=vv[:, ti, 65:129], s=pb[2][:, 64:128]:
                 e.tensor_copy(o, s), s_ev)

        s_nrm = b.sem(f"nrm{pair}")
        for (src, cell) in ((q_hp, 0), (k_hp, 1)):
            for t2_ in range(NTC):
                b.wait("scalar", s_ev)
                b.wait("scalar", s_mm)
                b.op("scalar", lambda e, o=st32a, s=src[:, t2_ * TC:(t2_ + 1) * TC]:
                     e.activation(o, s, AF.Square), s_nrm)
                b.wait("tensor", s_nrm)
                b.op("tensor", lambda e, o=pb[4][0:1, :], w=c["ones_c32"], r=st32a:
                     e.matmul(o, w, r, start=True, stop=True), s_mm)
                b.wait("vector", s_mm)
                if t2_ == 0:
                    b.op("vector", lambda e, o=mstat[0:1, cell:cell + 1], s=pb[4][0:1, :]:
                         e.reduce_max(o, s, AX.X), s_nrm)
                else:
                    b.op("vector", lambda e, o=mstat[0:1, 2:3], s=pb[4][0:1, :]:
                         e.reduce_max(o, s, AX.X), s_nrm)
                    b.wait("vector", s_nrm)
                    b.op("vector", lambda e, o=mstat[0:1, cell:cell + 1],
                         a=mstat[0:1, cell:cell + 1], bb=mstat[0:1, 2:3]:
                         e.tensor_tensor(o, a, bb, ALU.max), s_nrm)
        b.wait("scalar", s_nrm)
        b.op("scalar", lambda e, o=mstat[0:1, 0:1], s=mstat[0:1, 0:1]:
             e.activation(o, s, AF.Sqrt), s_nrm)
        b.op("scalar", lambda e, o=mstat[0:1, 1:2], s=mstat[0:1, 1:2]:
             e.activation(o, s, AF.Sqrt), s_nrm)
        b.wait("vector", s_nrm)
        b.op("vector", lambda e, o=mstat[0:1, 0:1], a=mstat[0:1, 0:1], bb=mstat[0:1, 1:2]:
             e.tensor_tensor(o, a, bb, ALU.mult), s_nrm)
        b.wait("vector", s_nrm)
        b.op("vector", lambda e, o=mstat[0:1, 0:1], a=mstat[0:1, 0:1]:
             e.tensor_scalar(o, a, -1.0, -1.0, ALU.mult, ALU.add), s_nrm)
        b.wait("tensor", s_nrm)
        emit_bcast_col(b, c, mstat[0:1, 0:1], bias_m, pb[4], s_mm, s_ev)
        b.wait("scalar", s_ev)

        for hh in range(2):
            h0 = hh * 64
            for qc in range(NTC):
                for blk in range(2):
                    exp_tok3 = {}
                    if blk > 0:
                        b.wait("scalar", s_pv)
                    for kj in range(16):
                        kt = blk * 16 + kj
                        bank = pb[kt % 2]
                        if kj >= 2:
                            b.wait("tensor", s_p, exp_tok3[kj - 2])
                        b.op("tensor", lambda e, o=bank,
                             w=k_hp[h0:h0 + 64, kt * 128:(kt + 1) * 128],
                             r=q_hp[h0:h0 + 64, qc * TC:(qc + 1) * TC]:
                             e.matmul(o, w, r, start=True, stop=True), s_mm)
                        b.wait("scalar", s_mm)
                        dst = p_buf[:, kj * 512:(kj + 1) * 512]
                        b.op("scalar", lambda e, o=dst, s=bank, bm=bias_m:
                             e.activation(o, s, AF.Exp, bias=bm), s_p)
                        exp_tok3[kj] = s_p.n
                    b.wait("tensor", s_p)
                    b.wait("tensor", s_ev)
                    b.wait("tensor", s_ob)
                    for kj in range(16):
                        kt = blk * 16 + kj
                        b.op("tensor", lambda e, o=pb[3][0:65, :],
                             w=vv[:, kt, 65 * hh:65 * hh + 65],
                             r=p_buf[:, kj * 512:(kj + 1) * 512],
                             st=(kt == 0), sp=(kt == NT - 1):
                             e.matmul(o, w, r, start=st, stop=sp),
                             s_pv if kj == 15 else None)
                b.wait("vector", s_pv)
                b.op("vector", lambda e, o=rrow[:, 0:512], s=pb[3][64:65, :]:
                     e.reciprocal(o, s), s_ev)
                b.wait("tensor", s_ev)
                b.op("tensor", lambda e, o=pb[4][0:64, :], w=c["ones_r32"][:, 0:64],
                     r=rrow[:, 0:512]:
                     e.matmul(o, w, r, start=True, stop=True), s_mm)
                b.wait("vector", s_mm)
                b.op("vector", lambda e, o=recb[0:64, :], s=pb[4][0:64, :]:
                     e.tensor_copy(o, s), s_ev)
                b.wait("vector", s_ev)
                dsto = o_all.rearrange("p (cc t) -> p cc t", cc=8)[
                    h0:h0 + 64, pair, qc * TC:(qc + 1) * TC]
                b.op("vector", lambda e, o=dsto, s=pb[3][0:64, :], r=recb[0:64, :]:
                     e.tensor_tensor(o, s, r, ALU.mult), s_ob)

    pjw = wka[:, 0:8 * 1024]
    b.wait("gpsimd", s_ob)
    b.dma("gpsimd", pjw.rearrange("p (cc k) -> p cc k", cc=8),
          projT.rearrange("(cc p) k -> p cc k", p=128), s_wld)
    b.wait("tensor", s_wld)
    b.wait("tensor", s_ob)
    b.wait("gpsimd", s_init)
    for dc in range(8):
        for tc in range(NTC):
            bank = pb[tc % 2]
            b.wait("tensor", s_ev)
            for cc2 in range(8):
                b.op("tensor", lambda e, o=bank,
                     w=pjw.rearrange("p (cc k) -> p cc k", cc=8)[:, cc2, dc * 128:(dc + 1) * 128],
                     r=o_all.rearrange("p (cc t) -> p cc t", cc=8)[:, cc2, tc * TC:(tc + 1) * TC],
                     st=(cc2 == 0), sp=(cc2 == 7):
                     e.matmul(o, w, r, start=st, stop=sp), s_mm if cc2 == 7 else None)
            emit_out_accum(b, c, bank, xo_d, dc, tc, st32a, st32tr, pb[5],
                           s_ev, s_tr, s_st, s_acc, s_mm_wait=(s_mm, s_mm.n),
                           bias_col=pjb_s[:, dc:dc + 1])
    attn_acc_total = s_acc.n

    if "ffn" not in parts:
        tap_map = {"xn_fm": xn_fm, "o_all": o_all, "q_hp": q_hp, "k_hp": k_hp,
                   "v_hp": v_hp, "bias_m": bias_m}
        for sm in (s_ev, s_ob, s_acc, s_mm, sA):
            b.wait("gpsimd", sm)
        s_tap = b.sem("tap")
        for name in taps:
            ap = tap_map[name]
            out_d = nc.dram_tensor("tap_" + name, list(ap.shape), ap.dtype,
                                   kind="ExternalOutput").ap()
            b.dma("gpsimd", out_d, ap, s_tap)
        b.wait("gpsimd", s_tap)
        return b, x_d

    cx = Cur(xn_fm)
    wg_r = cx.take(8 * 2816)
    wd_b = [cx.take(22 * 128) for _ in range(2)]
    stg = cx.take(1024)
    xqf = cx.take(1024)
    xqt = cx.take(1024, f32=True)
    co = Cur(o_all)
    wu_r = co.take(8 * 2816)
    qs = {}
    qs["am"] = co.take(176, f32=True)
    qs["sc"] = co.take(176, f32=True)
    qs["cc"] = co.take(2816)
    xnq = co.take(1024)
    qx = {"ax": co.take(1024, f32=True), "sg": co.take(1024),
          "cc": co.take(1024), "ct": co.take(1024),
          "am": co.take(64, f32=True), "sc": co.take(64, f32=True)}
    cw = Cur(wka)
    qs["ax"] = cw.take(2816, f32=True)
    qs["sg"] = cw.take(2816)
    gat = cw.take(2816, f32=True)
    wqout = cw.take(2816)
    qs["ct"] = wqout
    h_fm = cw.take(22 * 512)
    wtile = h_fm[:, 0:5632].bitcast(F32)

    s_q = b.sem("q")
    s_qa = b.sem("qa")
    s_q2 = b.sem("q2")
    s_qa2 = b.sem("qa2")
    s_tr2 = b.sem("tr2")
    s_ev2 = b.sem("ev2")
    s_wq = b.sem("wq")
    s_wr = b.sem("wr")
    s_wout = b.sem("wout")
    s_x2 = b.sem("lx2")

    b.wait("gpsimd", s_acc, attn_acc_total)

    def quant_weight(w_d, wT_s, rows, cols):
        ntile = rows // 128
        nb = cols // 16
        ncc = cols // 128
        for r in range(ntile):
            b.wait("sync", s_acc, attn_acc_total)
            b.wait("sync", s_q)
            b.wait("sync", s_st)
            b.wait("sync", s_qa)
            b.dma("sync", wtile[:, 0:cols], w_d[r * 128:(r + 1) * 128, :], s_wq)
            b.wait("vector", s_wq)
            b.wait("scalar", s_wq)
            b.wait("vector", s_tr)
            emit_quant(b, wtile[:, 0:cols], wqout[:, 0:cols], qs, s_q, s_qa, nb)
            b.wait("tensor", s_q)
            for g0 in range(0, ncc, 8):
                gn = min(8, ncc - g0)
                b.wait("tensor", s_st)
                for gj in range(gn):
                    b.op("tensor", lambda e,
                         o=pb[6].bitcast(BF16)[:, gj * 128:(gj + 1) * 128],
                         s=wqout[:, (g0 + gj) * 128:(g0 + gj + 1) * 128],
                         idt=c["ident"]:
                         e.transpose(o, s, idt), s_tr if gj == gn - 1 else None)
                b.wait("vector", s_tr)
                b.wait("vector", s_wout)
                b.op("vector", lambda e, o=stg[:, 0:gn * 128],
                     s=pb[6].bitcast(BF16)[:, 0:gn * 128]: e.tensor_copy(o, s), s_st)
                b.wait("gpsimd", s_st)
                dst = wT_s.rearrange("(cc p) r -> p cc r", p=128)[
                    :, g0:g0 + gn, r * 128:(r + 1) * 128]
                b.dma("gpsimd", dst,
                      stg[:, 0:gn * 128].rearrange("p (cc r) -> p cc r", cc=gn), s_wout)

    quant_weight(wg_d, wgT_s, 2816, 1024)
    if "prep1" in parts:
        b.wait("gpsimd", s_wout)
        b.wait("gpsimd", s_q)
        return b, x_d
    quant_weight(wu_d, wuT_s, 2816, 1024)
    quant_weight(wd_d, wdT_s, 1024, 2816)

    b.wait("gpsimd", s_wout)
    b.wait("gpsimd", s_q)
    b.dma("gpsimd", wg_r.rearrange("p (cc k) -> p cc k", cc=8),
          wgT_s.rearrange("(cc p) k -> p cc k", p=128), s_wr)
    b.dma("gpsimd", wu_r.rearrange("p (cc k) -> p cc k", cc=8),
          wuT_s.rearrange("(cc p) k -> p cc k", p=128), s_wr)
    b.wait("tensor", s_wr)

    s_wd = b.sem("wdl")
    for e_ in ("vector", "scalar"):
        b.wait(e_, s_q)
        b.wait(e_, s_qa)
    tok = {"qa2": 0, "tr2": 0, "xqf": 0, "gu_mm": 0,
           "htr": s_tr.n, "down_mm": 0}
    gev_tok = {0: None, 1: None}

    def prep_tile(ti):
        if ti == 0:
            b.wait("sync", s_mm)
        b.wait("sync", s_qa2, tok["qa2"])
        b.dma("sync", xqt, xn32_s[ti * 128:(ti + 1) * 128, :], s_x2)
        b.wait("scalar", s_x2)
        b.wait("vector", s_x2)
        b.wait("vector", s_tr2, tok["tr2"])
        emit_quant(b, xqt, xnq, qx, s_q2, s_qa2, 64)
        tok["qa2"] = s_qa2.n
        b.wait("tensor", s_q2)
        b.wait("tensor", s_ev2, tok["xqf"])
        for cc2 in range(8):
            b.op("tensor", lambda e,
                 o=pb[7].bitcast(BF16)[:, cc2 * 128:(cc2 + 1) * 128],
                 s=xnq[:, cc2 * 128:(cc2 + 1) * 128], idt=c["ident"]:
                 e.transpose(o, s, idt), s_tr2 if cc2 == 7 else None)
        tok["tr2"] = s_tr2.n
        b.wait("vector", s_tr2)
        b.wait("vector", s_mm, tok["gu_mm"])
        b.op("vector", lambda e, o=xqf.rearrange("p (c t) -> p c t", c=8),
             s=pb[7].bitcast(BF16).rearrange("p (c t) -> p c t", c=8):
             e.tensor_copy(o, s), s_ev2)
        tok["xqf"] = s_ev2.n

    def quant_half(lo, hi):
        qsl = {"ax": qs["ax"][:, lo * 16:hi * 16],
               "sg": qs["sg"][:, lo * 16:hi * 16],
               "cc": qs["cc"][:, lo * 16:hi * 16],
               "ct": qs["ct"][:, lo * 16:hi * 16],
               "am": qs["am"][:, lo:hi], "sc": qs["sc"][:, lo:hi]}
        b.wait("scalar", s_ev)
        b.wait("vector", s_tr, tok["htr"])
        b.wait("scalar", s_tr, tok["htr"])
        emit_quant(b, gat[:, lo * 16:hi * 16], wqout[:, lo * 16:hi * 16],
                   qsl, s_q, s_qa, hi - lo)

    prep_tile(0)
    for tc in range(NTC):
        for ti4 in range(4):
            ti = tc * 4 + ti4
            for fN in range(6):
                f0 = fN * 512
                f1 = min(f0 + 512, 2816)
                par = fN % 2
                bank_g = pb[2 * par]
                bank_u = pb[2 * par + 1]
                b.wait("tensor", s_ev2, tok["xqf"])
                if gev_tok[par] is not None:
                    b.wait("tensor", gev_tok[par][0], gev_tok[par][1])
                for (wres, bank) in ((wg_r, bank_g), (wu_r, bank_u)):
                    for cc2 in range(8):
                        b.op("tensor", lambda e, o=bank[:, 0:f1 - f0],
                             w=xqf.rearrange("p (c t) -> p c t", c=8)[:, cc2],
                             r=wres.rearrange("p (c k) -> p c k", c=8)[:, cc2, f0:f1],
                             st=(cc2 == 0), sp=(cc2 == 7):
                             e.matmul(o, w, r, start=st, stop=sp),
                             s_mm if cc2 == 7 else None)
                b.wait("scalar", s_mm)
                b.op("scalar", lambda e, o=gat[:, f0:f1], s=bank_g[:, 0:f1 - f0]:
                     e.activation(o, s, AF.Silu), s_qa)
                b.wait("vector", s_qa)
                b.op("vector", lambda e, o=gat[:, f0:f1], a=gat[:, f0:f1],
                     bb=bank_u[:, 0:f1 - f0]:
                     e.tensor_tensor(o, a, bb, ALU.mult), s_ev)
                gev_tok[par] = (s_ev, s_ev.n)
                if fN == 2:
                    quant_half(0, 88)
            tok["gu_mm"] = s_mm.n
            if ti < NT - 1:
                prep_tile(ti + 1)
            quant_half(88, 176)
            b.wait("tensor", s_q)
            for g0 in range(0, 22, 8):
                gn = min(8, 22 - g0)
                b.wait("tensor", s_st)
                for gj in range(gn):
                    b.op("tensor", lambda e,
                         o=pb[6].bitcast(BF16)[:, gj * 128:(gj + 1) * 128],
                         s=wqout[:, (g0 + gj) * 128:(g0 + gj + 1) * 128],
                         idt=c["ident"]:
                         e.transpose(o, s, idt), s_tr if gj == gn - 1 else None)
                b.wait("vector", s_tr)
                if ti4 == 0 and g0 == 0:
                    b.wait("vector", s_mm, tok["down_mm"])
                dsth = h_fm.rearrange("p (cc t) -> p cc t", cc=22)[
                    :, g0:g0 + gn, ti4 * 128:(ti4 + 1) * 128]
                b.op("vector", lambda e, o=dsth,
                     s=pb[6].bitcast(BF16).rearrange("p (g t) -> p g t", t=128)[:, 0:gn]:
                     e.tensor_copy(o, s), s_st)
            tok["htr"] = s_tr.n
        for dc in range(8):
            bank = pb[2 + dc % 2]
            bwd = wd_b[dc % 2]
            b.wait("gpsimd", s_mm)
            b.dma("gpsimd", bwd.rearrange("p (cc k) -> p cc k", cc=22),
                  wdT_s[:, dc * 128:(dc + 1) * 128].rearrange("(cc p) k -> p cc k", p=128),
                  s_wd)
            b.wait("tensor", s_wd)
            b.wait("tensor", s_st)
            b.wait("tensor", s_ev)
            for ci in range(22):
                b.op("tensor", lambda e, o=bank,
                     w=bwd.rearrange("p (cc k) -> p cc k", cc=22)[:, ci],
                     r=h_fm.rearrange("p (cc t) -> p cc t", cc=22)[:, ci],
                     st=(ci == 0), sp=(ci == 21):
                     e.matmul(o, w, r, start=st, stop=sp), s_mm if ci == 21 else None)
            emit_out_accum(b, c, bank, xo_d, dc, tc, st32a, st32tr, pb[5],
                           s_ev, s_tr, s_st, s_acc, s_mm_wait=(s_mm, s_mm.n),
                           wait_acc_tok=((s_acc, attn_acc_total)
                                         if (tc == 0 and dc == 0) else None))
        tok["down_mm"] = s_mm.n
        gev_tok[1] = (s_ev, s_ev.n)

    b.wait("gpsimd", s_acc)
    return b, x_d


def host_prep_l3(inputs):
    NH, DH = 16, 64
    qkv_w = np.asarray(inputs["L3_attn_qkv_w"], np.float32)
    qkv_b = np.asarray(inputs["L3_attn_qkv_b"], np.float32)
    perm = np.arange(3 * D)
    for sec in range(2):
        for h in range(NH):
            base = sec * D + h * DH
            idx = np.concatenate([np.arange(0, DH, 2), np.arange(1, DH, 2)])
            perm[base:base + DH] = base + idx
    qkv_wp = qkv_w[perm]
    qkv_bp = qkv_b[perm]
    return {
        "qkvT": np.ascontiguousarray(qkv_wp.T),
        "qkvb": np.ascontiguousarray(qkv_bp.reshape(1, 3072)),
        "projT": np.ascontiguousarray(np.asarray(inputs["L3_attn_proj_w"], np.float32).T),
        "projb": np.ascontiguousarray(
            np.asarray(inputs["L3_attn_proj_b"], np.float32).reshape(8, 128).T),
        "wg": np.asarray(inputs["L3_ffn_w_gate"], np.float32),
        "wu": np.asarray(inputs["L3_ffn_w_up"], np.float32),
        "wd": np.asarray(inputs["L3_ffn_w_down"], np.float32),
    }


def run_l3(x_full, inputs):
    b, _ = build_l3()
    nc = b.finish()
    w = host_prep_l3(inputs)
    in_maps = []
    for core in range(8):
        m = dict(w)
        m["x"] = np.ascontiguousarray(x_full[core], dtype=np.float32)
        in_maps.append(m)
    res = run_bass_kernel_spmd(nc, in_maps, core_ids=list(range(8)))
    return np.stack([res.results[i]["xo"] for i in range(8)], axis=0)



def build_l2(b=None, x_d=None, xo_d=None, pfx=""):
    if b is None:
        b = B()
    nc = b.nc
    DI = 2048
    if x_d is None:
        x_d = nc.dram_tensor("x", [T, D], F32, kind="ExternalInput").ap()
    winT = nc.dram_tensor(pfx + "winT", [D, 2 * DI], F32, kind="ExternalInput").ap()
    convw = nc.dram_tensor(pfx + "convw", [128, 64], F32, kind="ExternalInput").ap()
    convb = nc.dram_tensor(pfx + "convb", [128, 16], F32, kind="ExternalInput").ap()
    wxT = nc.dram_tensor(pfx + "wxT", [DI, 2080], F32, kind="ExternalInput").ap()
    dcol = nc.dram_tensor(pfx + "dcol", [128, 16], F32, kind="ExternalInput").ap()
    woutT = nc.dram_tensor(pfx + "woutT", [DI, D], F32, kind="ExternalInput").ap()
    w1_d = nc.dram_tensor(pfx + "w1", [8, D, 4096], F32, kind="ExternalInput").ap()
    b1_d = nc.dram_tensor(pfx + "b1", [8, 128, 32], F32, kind="ExternalInput").ap()
    w2_d = nc.dram_tensor(pfx + "w2", [8, DI, D], F32, kind="ExternalInput").ap()
    b2_d = nc.dram_tensor(pfx + "b2", [8, 128, 8], F32, kind="ExternalInput").ap()
    wr_d = nc.dram_tensor(pfx + "wr", [D, 8], F32, kind="ExternalInput").ap()
    br_d = nc.dram_tensor(pfx + "br", [1, 8], F32, kind="ExternalInput").ap()
    if xo_d is None:
        xo_d = nc.dram_tensor("xo", [T, D], F32, kind="ExternalOutput").ap()
    facc_d = nc.dram_tensor("facc", [8 * 128, 4096], F32, kind="Internal").ap()

    c = load_consts(b)
    sel_np = np.zeros((8, 8 * 128), dtype=BF)
    for e_ in range(8):
        sel_np[e_, e_ * 128:(e_ + 1) * 128] = 1.0
    d_, sel8 = b.const("sel8", sel_np)
    b.dma("sync", sel8, d_, c["ld"])

    xn_fm = b.sb("xn_fm", [128, 8 * T], BF16)
    wkb = b.sb("wkb", [128, 53504], BF16)
    wk32 = b.sb("wk32", [128, 2496], F32)
    pb = [b.ps(f"pb{i}", [128, 512], F32) for i in range(8)]

    s_mm = b.sem("mm")
    s_ev = b.sem("ev")
    s_init = b.sem("init")
    s_acc = b.sem("acc")
    s_tr = b.sem("tr")
    s_st = b.sem("st")
    s_wld = b.sem("wld")
    s_ch = b.sem("ch")
    s_cha = b.sem("cha")

    b.dma("gpsimd", xo_d, x_d, s_init)

    cb = Cur(wkb)
    xb_buf = cb.take(16 * 515 + 16)
    sz_buf = cb.take(16 * 512)
    xc_buf = cb.take(16 * 512)
    dl_buf = cb.take(16 * 512)
    yz_buf = cb.take(16 * 512)
    wstr = [cb.take(8 * 128) for _ in range(2)]
    wstr2 = [cb.take(16 * 128) for _ in range(2)]
    wbc = cb.take(16 * 32)
    ln_bufs = (sz_buf[:, 0:2048].bitcast(F32), sz_buf[:, 2048:4096].bitcast(F32),
               xc_buf[:, 0:1024], xc_buf[:, 1024:2048], xc_buf[:, 2048:3072])
    st32a = wk32[:, 0:512]
    st32tr = wk32[:, 512:1024]
    bc_b = wk32[:, 1024:1536]
    st32b = wk32[:, 1536:2048]
    brow_s16 = b.sb("brow_s16", [1, 8], BF16)
    bc_row = yz_buf[0:1, 0:1024].bitcast(F32)
    convw_s = wk32[:, 2048:2112]
    convb_s = wk32[:, 2112:2128]
    dcol_s = wk32[:, 2128:2144]
    brow_s = wk32[0:1, 2144:2152]

    b.wait("sync", c["ld"])
    sA = emit_ln(b, c, x_d, xn_fm, pb=[pb[6], pb[7]], bufs=ln_bufs)

    s_wc = b.sem("wc")
    b.dma("gpsimd", convw_s, convw, s_wc)
    b.dma("gpsimd", convb_s, convb, s_wc)
    b.dma("gpsimd", dcol_s, dcol, s_wc)
    b.dma("gpsimd", brow_s, br_d, s_wc)
    b.dma("gpsimd", brow_s16, br_d, s_wc)
    b.dma("gpsimd", wbc.rearrange("p (cc k) -> p cc k", cc=16),
          wxT[:, 2048:2080].rearrange("(cc p) k -> p cc k", p=128), s_wc)

    for e_ in ("tensor", "scalar", "vector"):
        b.wait(e_, sA)
    b.wait("tensor", c["ld"])
    b.wait("tensor", s_wc)
    b.wait("vector", s_wc)

    s_w1 = b.sem("w1s")
    w1_tok = [0, 0]

    for tc in range(NTC):
        if tc == 0:
            b.op("vector", lambda e, o=xb_buf[:, 0:8240].rearrange("p (cc k) -> p cc k", k=515)[:, :, 0:3]:
                 e.memset(o, 0.0), s_ch)
        else:
            b.wait("vector", s_ch)
            b.op("vector", lambda e,
                 o=xb_buf[:, 0:8240].rearrange("p (cc k) -> p cc k", k=515)[:, :, 0:3],
                 s=xb_buf[:, 0:8240].rearrange("p (cc k) -> p cc k", k=515)[:, :, 512:515]:
                 e.tensor_copy(o, s), s_ch)
        for fc in range(32):
            par = fc % 2
            bw = wstr[par]
            b.wait("gpsimd", s_mm, w1_tok[par])
            b.dma("gpsimd", bw.rearrange("p (cc k) -> p cc k", cc=8),
                  winT[:, fc * 128:(fc + 1) * 128].rearrange("(cc p) k -> p cc k", p=128),
                  s_w1)
            bank = pb[par]
            b.wait("tensor", s_w1)
            b.wait("tensor", s_ev)
            b.wait("tensor", s_ch)
            for cc2 in range(8):
                b.op("tensor", lambda e, o=bank,
                     w=bw.rearrange("p (cc k) -> p cc k", cc=8)[:, cc2],
                     r=xn_view(xn_fm, cc2, tc * TC, (tc + 1) * TC),
                     st=(cc2 == 0), sp=(cc2 == 7):
                     e.matmul(o, w, r, start=st, stop=sp), s_mm if cc2 == 7 else None)
            w1_tok[par] = s_mm.n
            if fc < 16:
                b.wait("vector", s_mm)
                dst = xb_buf[:, 0:8240].rearrange("p (cc k) -> p cc k", k=515)[:, fc, 3:515]
                b.op("vector", lambda e, o=dst, s=bank: e.tensor_copy(o, s), s_ev)
            else:
                b.wait("scalar", s_mm)
                dst = sz_buf[:, (fc - 16) * 512:(fc - 15) * 512]
                b.op("scalar", lambda e, o=dst, s=bank: e.activation(o, s, AF.Silu), s_ev)
        b.wait("vector", s_ev)
        for cc2 in range(16):
            xv3 = xb_buf[:, 0:8240].rearrange("p (cc k) -> p cc k", k=515)
            b.op("vector", lambda e, o=st32a, s=xv3[:, cc2, 0:512],
                 w=convw_s[:, cc2 * 4:cc2 * 4 + 1], bb=convb_s[:, cc2:cc2 + 1]:
                 e.tensor_scalar(o, s, w, bb, ALU.mult, ALU.add), s_ch)
            for k_ in range(1, 4):
                b.wait("vector", s_ch)
                b.op("vector", lambda e, o=st32a, s=xv3[:, cc2, k_:k_ + 512],
                     w=convw_s[:, cc2 * 4 + k_:cc2 * 4 + k_ + 1], a=st32a:
                     e.scalar_tensor_tensor(o, s, w, a, ALU.mult, ALU.add), s_ch)
            b.wait("scalar", s_ch)
            b.op("scalar", lambda e, o=xc_buf[:, cc2 * 512:(cc2 + 1) * 512], s=st32a:
                 e.activation(o, s, AF.Silu), s_cha)
            b.wait("vector", s_cha)
        for fc in range(16):
            par = fc % 2
            bw = wstr2[par]
            b.wait("gpsimd", s_mm, w1_tok[par])
            b.dma("gpsimd", bw.rearrange("p (cc k) -> p cc k", cc=16),
                  wxT[:, fc * 128:(fc + 1) * 128].rearrange("(cc p) k -> p cc k", p=128),
                  s_w1)
            bank = pb[par]
            b.wait("tensor", s_w1)
            b.wait("tensor", s_cha)
            b.wait("tensor", s_ch)
            for cc2 in range(16):
                b.op("tensor", lambda e, o=bank,
                     w=bw.rearrange("p (cc k) -> p cc k", cc=16)[:, cc2],
                     r=xc_buf[:, cc2 * 512:(cc2 + 1) * 512],
                     st=(cc2 == 0), sp=(cc2 == 15):
                     e.matmul(o, w, r, start=st, stop=sp), s_mm if cc2 == 15 else None)
            w1_tok[par] = s_mm.n
            b.wait("scalar", s_mm)
            b.wait("scalar", s_ch)
            b.op("scalar", lambda e, o=st32b, s=bank: e.activation(o, s, AF.Abs), s_cha)
            b.wait("scalar", s_cha)
            b.op("scalar", lambda e, o=st32b, s=st32b: e.activation(o, s, AF.Exp, scale=-1.0), s_cha)
            b.wait("scalar", s_cha)
            b.op("scalar", lambda e, o=st32b, s=st32b: e.activation(o, s, AF.Ln, bias=1.0), s_cha)
            b.wait("vector", s_cha)
            b.wait("vector", s_mm)
            b.op("vector", lambda e, o=st32a, s=bank: e.tensor_scalar(
                o, s, 0.0, None, ALU.max), s_ch)
            b.wait("vector", s_ch)
            b.op("vector", lambda e, o=dl_buf[:, fc * 512:(fc + 1) * 512],
                 a=st32a, bb=st32b: e.tensor_tensor(o, a, bb, ALU.add), s_ch)
        b.wait("tensor", s_ch)
        for ti4 in range(4):
            bank = pb[4]
            for cc2 in range(16):
                b.op("tensor", lambda e, o=bank[:, 0:32],
                     w=xc_buf[:, cc2 * 512 + ti4 * 128: cc2 * 512 + (ti4 + 1) * 128],
                     r=wbc.rearrange("p (cc k) -> p cc k", cc=16)[:, cc2],
                     st=(cc2 == 0), sp=(cc2 == 15):
                     e.matmul(o, w, r, start=st, stop=sp), s_mm if cc2 == 15 else None)
            b.wait("vector", s_mm)
            b.op("vector", lambda e, o=st32b[:, 0:32], s=bank[:, 0:32]:
                 e.tensor_copy(o, s), s_ch)
            b.wait("vector", s_ch)
            b.op("vector", lambda e, o=st32b[:, 32:48], a=st32b[:, 0:16],
                 bb=st32b[:, 16:32]: e.tensor_tensor(o, a, bb, ALU.mult), s_ch)
            b.wait("vector", s_ch)
            b.op("vector", lambda e, o=st32b[:, 48:49], s=st32b[:, 32:48]:
                 e.reduce_sum(o, s, AX.X), s_ch)
            b.wait("tensor", s_ch)
            b.op("tensor", lambda e, o=pb[5][0:1, 0:128], s=st32b[:, 48:49],
                 idt=c["ident32"]: e.transpose(o, s, idt), s_tr)
            b.wait("vector", s_tr)
            b.op("vector", lambda e, o=bc_row[0:1, ti4 * 128:(ti4 + 1) * 128],
                 s=pb[5][0:1, 0:128]: e.tensor_copy(o, s), s_ch)
        b.wait("tensor", s_ch)
        b.op("tensor", lambda e, o=pb[5], w=c["ones_r32"][:, 0:128], r=bc_row[0:1, 0:512]:
             e.matmul(o, w, r, start=True, stop=True), s_mm)
        b.wait("vector", s_mm)
        b.op("vector", lambda e, o=bc_b, s=pb[5]: e.tensor_copy(o, s), s_ch)
        b.wait("vector", s_ch)
        for cc2 in range(16):
            b.op("vector", lambda e, o=st32a, a=dl_buf[:, cc2 * 512:(cc2 + 1) * 512],
                 bb=xc_buf[:, cc2 * 512:(cc2 + 1) * 512]:
                 e.tensor_tensor(o, a, bb, ALU.mult), s_ch)
            b.wait("vector", s_ch)
            b.op("vector", lambda e, o=st32a, a=st32a, bb=bc_b:
                 e.tensor_tensor(o, a, bb, ALU.mult), s_ch)
            b.wait("vector", s_ch)
            b.op("vector", lambda e, o=st32a, s=xc_buf[:, cc2 * 512:(cc2 + 1) * 512],
                 w=dcol_s[:, cc2:cc2 + 1], a=st32a:
                 e.scalar_tensor_tensor(o, s, w, a, ALU.mult, ALU.add), s_ch)
            b.wait("vector", s_ch)
            b.op("vector", lambda e, o=yz_buf[:, cc2 * 512:(cc2 + 1) * 512],
                 a=st32a, bb=sz_buf[:, cc2 * 512:(cc2 + 1) * 512]:
                 e.tensor_tensor(o, a, bb, ALU.mult), s_ch)
            b.wait("vector", s_ch)
        b.wait("gpsimd", s_init)
        for dc in range(8):
            par = dc % 2
            bw = wstr2[par]
            b.wait("gpsimd", s_mm, w1_tok[par])
            b.dma("gpsimd", bw.rearrange("p (cc k) -> p cc k", cc=16),
                  woutT[:, dc * 128:(dc + 1) * 128].rearrange("(cc p) k -> p cc k", p=128),
                  s_w1)
            bank = pb[2 + par]
            b.wait("tensor", s_w1)
            b.wait("tensor", s_ev)
            b.wait("tensor", s_ch)
            for cc2 in range(16):
                b.op("tensor", lambda e, o=bank,
                     w=bw.rearrange("p (cc k) -> p cc k", cc=16)[:, cc2],
                     r=yz_buf[:, cc2 * 512:(cc2 + 1) * 512],
                     st=(cc2 == 0), sp=(cc2 == 15):
                     e.matmul(o, w, r, start=st, stop=sp), s_mm if cc2 == 15 else None)
            w1_tok[par] = s_mm.n
            emit_out_accum(b, c, bank, xo_d, dc, tc, st32a, st32tr, pb[5],
                           s_ev, s_tr, s_st, s_acc, s_mm_wait=(s_mm, s_mm.n))
    attn_acc_total = s_acc.n

    wrt_tm = wk32[:, 1024:1536]
    wrt_tm = wk32[:, 1024:1280]
    w8T = b.sb("w8T", [8, 4096], BF16)
    wr_s = b.sb("wr_s", [128, 8 * 8], BF16)
    b1_s = b.sb("b1_s", [128, 32], F32)
    b2_s = b.sb("b2_s", [128, 8], F32)
    s_rt = b.sem("rt")
    b.dma("gpsimd", wr_s.rearrange("p (cc k) -> p cc k", cc=8),
          wr_d.rearrange("(cc p) k -> p cc k", p=128), s_rt)
    b.wait("tensor", s_rt)
    lgs = wk32[:, 1280:1296]
    for ti in range(NT):
        bank = pb[4]
        b.wait("tensor", s_ch)
        for cc2 in range(8):
            b.op("tensor", lambda e, o=bank[:, 0:8],
                 w=xn_view(xn_fm, cc2, ti * 128, (ti + 1) * 128),
                 r=wr_s.rearrange("p (cc k) -> p cc k", cc=8)[:, cc2],
                 st=(cc2 == 0):
                 e.matmul(o, w, r, start=st, stop=False))
        b.op("tensor", lambda e, o=bank[:, 0:8], w=c["ones_r16"][:, 0:128],
             r=brow_s16: e.matmul(o, w, r, start=False, stop=True), s_mm)
        b.wait("vector", s_mm)
        lg = lgs[:, 0:8]
        m1 = wk32[:, 1296:1297]
        m2 = wk32[:, 1297:1298]
        eq1 = wk32[:, 1300:1308]
        eq2 = wk32[:, 1308:1316]
        t2_ = wk32[:, 1316:1324]
        ee = wk32[:, 1298:1299]
        s1c = wk32[:, 1324:1325]
        s2c = wk32[:, 1325:1326]
        b.op("vector", lambda e, o=lg, s=bank[:, 0:8]: e.tensor_copy(o, s), s_ch)
        b.wait("vector", s_ch)
        b.op("vector", lambda e, o=m1, s=lg: e.reduce_max(o, s, AX.X), s_ch)
        b.wait("vector", s_ch)
        b.op("vector", lambda e, o=eq1, s=lg, w=m1: e.tensor_scalar(
            o, s, w, None, ALU.is_equal), s_ch)
        b.wait("vector", s_ch)
        b.op("vector", lambda e, o=t2_, s=eq1, a=lg: e.scalar_tensor_tensor(
            o, s, -1e30, a, ALU.mult, ALU.add), s_ch)
        b.wait("vector", s_ch)
        b.op("vector", lambda e, o=m2, s=t2_: e.reduce_max(o, s, AX.X), s_ch)
        b.wait("vector", s_ch)
        b.op("vector", lambda e, o=eq2, s=t2_, w=m2: e.tensor_scalar(
            o, s, w, None, ALU.is_equal), s_ch)
        b.op("vector", lambda e, o=ee, a=m2, bb=m1: e.tensor_tensor(
            o, a, bb, ALU.subtract), s_ch)
        b.wait("scalar", s_ch)
        b.op("scalar", lambda e, o=ee, s=ee: e.activation(o, s, AF.Exp), s_cha)
        b.wait("vector", s_cha)
        b.op("vector", lambda e, o=s1c, s=ee: e.tensor_scalar(o, s, 1.0, None, ALU.add), s_ch)
        b.wait("vector", s_ch)
        b.op("vector", lambda e, o=s1c, s=s1c: e.reciprocal(o, s), s_ch)
        b.wait("vector", s_ch)
        b.op("vector", lambda e, o=s2c, a=ee, bb=s1c: e.tensor_tensor(
            o, a, bb, ALU.mult), s_ch)
        b.wait("vector", s_ch)
        b.op("vector", lambda e, o=eq1, s=eq1, w=s1c: e.tensor_scalar(
            o, s, w, None, ALU.mult), s_ch)
        b.wait("vector", s_ch)
        b.op("vector", lambda e, o=wrt_tm[:, ti * 8:(ti + 1) * 8], s=eq2, w=s2c,
             a=eq1: e.scalar_tensor_tensor(o, s, w, a, ALU.mult, ALU.add), s_ch)
        b.wait("vector", s_ch)
        b.wait("tensor", s_ch)
        b.wait("tensor", s_st)
        b.op("tensor", lambda e, o=pb[5][0:8, 0:128], s=wrt_tm[:, ti * 8:(ti + 1) * 8],
             idt=c["ident32"]: e.transpose(o, s, idt), s_tr)
        b.wait("vector", s_tr)
        b.op("vector", lambda e, o=w8T[:, ti * 128:(ti + 1) * 128],
             s=pb[5][0:8, 0:128]: e.tensor_copy(o, s), s_st)

    c2 = Cur(wkb)
    act_b = c2.take(16 * 2048)
    w2_r = c2.take(16 * 1024)
    w1t = [c2.take(2 * 8 * 128) for _ in range(2)]
    wb_sb = wk32[:, 1536:2048]
    s_we = b.sem("we")
    s_w1t = b.sem("w1t")
    s_fa = b.sem("fa")
    s_fad = b.sem("fad")
    w1t_tok = [0, 0]
    cha_tok = [0, 0]
    fa_tok = [0, 0]
    stt_tok = [0]

    b.wait("gpsimd", s_acc, attn_acc_total)
    for e_ in range(8):
        b.wait("gpsimd", s_mm)
        b.wait("gpsimd", s_ch)
        b.wait("gpsimd", s_cha)
        b.dma("gpsimd", w2_r.rearrange("p (cc k) -> p cc k", cc=16),
              w2_d[e_].rearrange("(cc p) k -> p cc k", p=128), s_we)
        b.dma("gpsimd", b1_s, b1_d[e_], s_we)
        b.dma("gpsimd", b2_s, b2_d[e_], s_we)
        b.wait("tensor", s_we)
        for half in range(2):
            tcs = range(half * 4, half * 4 + 4)
            for j in range(16):
                par = j % 2
                bw = w1t[par]
                b.wait("gpsimd", s_mm, w1t_tok[par])
                for (gi, col0) in ((0, j * 128), (1, 2048 + j * 128)):
                    b.dma("gpsimd",
                          bw.rearrange("p (g cc k) -> p g cc k", g=2, cc=8)[:, gi],
                          w1_d[e_][:, col0:col0 + 128].rearrange("(cc p) k -> p cc k", p=128),
                          s_w1t)
                b.wait("tensor", s_w1t)
                for tci, tc in enumerate(tcs):
                    parb = tci % 2
                    bank_a = pb[0] if parb == 0 else pb[6]
                    bank_g = pb[1] if parb == 0 else pb[7]
                    b.wait("tensor", s_cha, cha_tok[parb])
                    for (gi, bank) in ((0, bank_a), (1, bank_g)):
                        for cc2 in range(8):
                            b.op("tensor", lambda e, o=bank,
                                 w=bw.rearrange("p (g cc k) -> p g cc k", g=2, cc=8)[:, gi, cc2],
                                 r=xn_view(xn_fm, cc2, tc * TC, (tc + 1) * TC),
                                 st=(cc2 == 0), sp=(cc2 == 7):
                                 e.matmul(o, w, r, start=st, stop=sp),
                                 s_mm if cc2 == 7 else None)
                    b.wait("scalar", s_mm)
                    b.wait("scalar", s_cha, stt_tok[0])
                    b.op("scalar", lambda e, o=st32a, s=bank_a, bc=b1_s[:, j:j + 1]:
                         e.activation(o, s, AF.Silu, bias=bc), s_cha)
                    b.wait("vector", s_cha)
                    b.op("vector", lambda e,
                         o=act_b[:, j * 2048 + tci * 512: j * 2048 + (tci + 1) * 512],
                         s=bank_g, w=b1_s[:, 16 + j:17 + j], a=st32a:
                         e.scalar_tensor_tensor(o, s, w, a, ALU.add, ALU.mult), s_cha)
                    cha_tok[parb] = s_cha.n
                    stt_tok[0] = s_cha.n
                w1t_tok[par] = s_mm.n
            for tci, tc in enumerate(tcs):
                bank4 = pb[4]
                b.wait("tensor", s_ch)
                b.op("tensor", lambda e, o=bank4, w=sel8[:, e_ * 128:(e_ + 1) * 128],
                     r=w8T[:, tc * TC:(tc + 1) * TC]:
                     e.matmul(o, w, r, start=True, stop=True), s_mm)
                b.wait("vector", s_mm)
                b.op("vector", lambda e, o=wb_sb, s=bank4: e.tensor_copy(o, s), s_ch)
                for dc in range(8):
                    par = dc % 2
                    bank = pb[2 + par]
                    b.wait("tensor", s_cha)
                    b.wait("tensor", s_fa, fa_tok[par])
                    for cc2 in range(16):
                        b.op("tensor", lambda e, o=bank,
                             w=w2_r.rearrange("p (cc k) -> p cc k", cc=16)[:, cc2, dc * 128:(dc + 1) * 128],
                             r=act_b[:, cc2 * 2048 + tci * 512: cc2 * 2048 + (tci + 1) * 512],
                             st=(cc2 == 0), sp=(cc2 == 15):
                             e.matmul(o, w, r, start=st, stop=sp),
                             s_mm if cc2 == 15 else None)
                    b.wait("vector", s_mm)
                    b.wait("vector", s_ch)
                    b.wait("vector", s_fad)
                    b.op("vector", lambda e, o=st32tr, s=bank, w=b2_s[:, dc:dc + 1],
                         a=wb_sb: e.scalar_tensor_tensor(o, s, w, a, ALU.add, ALU.mult), s_fa)
                    fa_tok[par] = s_fa.n
                    b.wait("gpsimd", s_fa)
                    dstf = facc_d[dc * 128:(dc + 1) * 128, tc * TC:(tc + 1) * TC]
                    if e_ == 0:
                        b.dma("gpsimd", dstf, st32tr, s_fad)
                    else:
                        b.dma("gpsimd", dstf, st32tr, s_fad, accum_op=ALU.add)

    s_fl = b.sem("fl")
    fstage = wkb[:, 0:1024].bitcast(F32)
    for dc in range(8):
        for tc in range(NTC):
            b.wait("sync", s_ch)
            b.wait("sync", s_fa)
            b.wait("sync", s_fad)
            b.wait("sync", s_tr)
            b.dma("sync", fstage[:, 0:512],
                  facc_d[dc * 128:(dc + 1) * 128, tc * TC:(tc + 1) * TC], s_fl)
            b.wait("tensor", s_fl)
            b.wait("tensor", s_st)
            for j in range(4):
                b.op("tensor", lambda e, o=pb[5][:, j * 128:(j + 1) * 128],
                     s=fstage[:, j * 128:(j + 1) * 128], idt=c["ident32"]:
                     e.transpose(o, s, idt), s_tr if j == 3 else None)
            b.wait("vector", s_tr)
            b.wait("vector", s_acc)
            b.wait("vector", s_fad)
            b.op("vector", lambda e, o=st32tr, s=pb[5][:, 0:512]: e.tensor_copy(o, s), s_st)
            b.wait("gpsimd", s_st)
            dst = xo_d[tc * TC:(tc + 1) * TC, dc * 128:(dc + 1) * 128]
            b.dma("gpsimd", dst.rearrange("(j q) d -> q j d", j=4),
                  st32tr.rearrange("q (j d) -> q j d", j=4), s_acc, accum_op=ALU.add)
    b.wait("gpsimd", s_acc)
    b.wait("gpsimd", s_fad)
    return b, x_d


def host_prep_l2(inputs):
    t = lambda a: np.ascontiguousarray(np.asarray(a, np.float32).T)
    wx = np.asarray(inputs["L2_attn_W_xproj"], np.float32)
    perm = np.concatenate([np.arange(32, 2080), np.arange(0, 32)])
    wxT = np.ascontiguousarray(wx[perm].T)
    conv_w = np.asarray(inputs["L2_attn_conv_w"], np.float32)
    return {
        "winT": t(inputs["L2_attn_W_in"]),
        "convw": np.ascontiguousarray(conv_w.reshape(16, 128, 4).transpose(1, 0, 2).reshape(128, 64)),
        "convb": np.ascontiguousarray(np.asarray(inputs["L2_attn_conv_b"], np.float32).reshape(16, 128).T),
        "wxT": wxT,
        "dcol": np.ascontiguousarray(np.asarray(inputs["L2_attn_D"], np.float32).reshape(16, 128).T),
        "woutT": t(inputs["L2_attn_W_out"]),
        "w1": np.asarray(inputs["L2_ffn_W1"], np.float32),
        "b1": np.ascontiguousarray(np.stack([np.asarray(inputs["L2_ffn_b1"], np.float32)[e].reshape(32, 128).T for e in range(8)])),
        "w2": np.asarray(inputs["L2_ffn_W2"], np.float32),
        "b2": np.ascontiguousarray(np.stack([np.asarray(inputs["L2_ffn_b2"], np.float32)[e].reshape(8, 128).T for e in range(8)])),
        "wr": np.asarray(inputs["L2_ffn_Wr"], np.float32),
        "br": np.asarray(inputs["L2_ffn_br"], np.float32).reshape(1, 8),
    }


def run_l2(x_full, inputs):
    b, _ = build_l2()
    nc = b.finish()
    w = host_prep_l2(inputs)
    in_maps = []
    for core in range(8):
        m = dict(w)
        m["x"] = np.ascontiguousarray(x_full[core], dtype=np.float32)
        in_maps.append(m)
    res = run_bass_kernel_spmd(nc, in_maps, core_ids=list(range(8)))
    return np.stack([res.results[i]["xo"] for i in range(8)], axis=0)



def build_l1(parts=("attn", "ffn"), taps=(), b=None, x_d=None, xo_d=None, pfx=""):
    if b is None:
        b = B()
    nc = b.nc
    if x_d is None:
        x_d = nc.dram_tensor("x", [T, D], F32, kind="ExternalInput").ap()
    wqT = nc.dram_tensor(pfx + "wqT", [D, D], F32, kind="ExternalInput").ap()
    wkT = nc.dram_tensor(pfx + "wkT", [D, D], F32, kind="ExternalInput").ap()
    wvT = nc.dram_tensor(pfx + "wvT", [D, D], F32, kind="ExternalInput").ap()
    woT = nc.dram_tensor(pfx + "woT", [D, D], F32, kind="ExternalInput").ap()
    wb_d = nc.dram_tensor(pfx + "wb", [D, 16], F32, kind="ExternalInput").ap()
    bb_d = nc.dram_tensor(pfx + "bb", [1, 16], F32, kind="ExternalInput").ap()
    phiT = nc.dram_tensor(pfx + "phiT", [D, 2048], F32, kind="ExternalInput").ap()
    phib_d = nc.dram_tensor(pfx + "phib", [1, 2048], F32, kind="ExternalInput").ap()
    ew_d = nc.dram_tensor(pfx + "ew", [8, D, D], F32, kind="ExternalInput").ap()
    eb_d = nc.dram_tensor(pfx + "eb", [8, D], F32, kind="ExternalInput").ap()
    if xo_d is None:
        xo_d = nc.dram_tensor("xo", [T, D], F32, kind="ExternalOutput").ap()
    xt_part = nc.dram_tensor("xt_part", [2048, D], F32, kind="Internal").ap()
    xt_red = nc.dram_tensor("xt_red", [256, D], F32, kind="Internal").ap()
    yt_in = nc.dram_tensor("yt_in", [256, D], F32, kind="Internal").ap()
    yt_full = nc.dram_tensor("yt_full", [2048, D], F32, kind="Internal",
                             addr_space="Shared").ap()
    c_scr = nc.dram_tensor("c_scr", [2048, T], BF16, kind="Internal").ap()
    RG = [list(range(8))]

    c = load_consts(b)
    mask_np = np.zeros((4, 128, 512), dtype=BF)
    for r in range(4):
        for kk in range(128):
            q0 = r * 128 + kk
            if q0 < 512:
                mask_np[r, kk, q0:] = 1.0
    mask_d = nc.inline_tensor(mask_np.reshape(4 * 128, 512), name="c_l1mask").ap()
    masks = b.sb("s_l0mask", [128, 4 * 512], BF16)
    for r in range(4):
        b.dma("sync", masks[:, r * 512:(r + 1) * 512],
              mask_d[r * 128:(r + 1) * 128, :], c["ld"])

    xn_fm = b.sb("xn_fm", [128, 8 * T], BF16)
    o_all = b.sb("o_all", [128, 8 * T], BF16)
    wkc = b.sb("wkc", [128, 29440], BF16)
    wk32 = b.sb("wk32", [128, 2496], F32)
    pb = [b.ps(f"pb{i}", [128, 512], F32) for i in range(8)]

    s_mm = b.sem("mm")
    s_ev = b.sem("ev")
    s_init = b.sem("init")
    s_acc = b.sem("acc")
    s_tr = b.sem("tr")
    s_st = b.sem("st")
    s_ch = b.sem("ch")
    s_cha = b.sem("cha")
    s_cc = b.sem("cc")

    b.dma("gpsimd", xo_d, x_d, s_init)

    cc_ = Cur(wkc)
    qh = cc_.take(T)
    kh = cc_.take(T)
    vh = cc_.take(32 * 128)
    p_buf = cc_.take(8 * 1024)
    wq_p = p_buf[:, 0:3 * 1024]
    beta_tm = cc_.take(32 * 16)
    st16 = cc_.take(128)
    st32a = wk32[:, 0:512]
    st32tr = wk32[:, 512:1024]
    sqv = wk32[:, 1024:1152]
    nrm2 = wk32[:, 1152:1156]
    ln_a = b.sb("ln_a1", [128, 8], F32)

    ln_bufs = (qh[:, 0:2048].bitcast(F32), qh[:, 2048:4096].bitcast(F32),
               kh[:, 0:1024], kh[:, 1024:2048], kh[:, 2048:3072])
    b.wait("sync", c["ld"])
    sA = emit_ln(b, c, x_d, xn_fm, pb=[pb[6], pb[7]], bufs=ln_bufs)

    for e_ in ("tensor", "scalar", "vector"):
        b.wait(e_, sA)
    b.wait("tensor", c["ld"])

    wbb_s = b.sb("wbb_s", [128, 8 * 16], BF16)
    bbr_s = b.sb("bbr_s", [1, 16], BF16)
    s_wb = b.sem("wbl")
    b.dma("gpsimd", wbb_s.rearrange("p (cc k) -> p cc k", cc=8),
          wb_d.rearrange("(cc p) k -> p cc k", p=128), s_wb)
    b.dma("gpsimd", bbr_s, bb_d, s_wb)
    b.wait("tensor", s_wb)
    for ti in range(NT):
        bank = pb[4]
        b.wait("tensor", s_cha)
        for cc2 in range(8):
            b.op("tensor", lambda e, o=bank[:, 0:16],
                 w=xn_view(xn_fm, cc2, ti * 128, (ti + 1) * 128),
                 r=wbb_s.rearrange("p (cc k) -> p cc k", cc=8)[:, cc2],
                 st=(cc2 == 0):
                 e.matmul(o, w, r, start=st, stop=False))
        b.op("tensor", lambda e, o=bank[:, 0:16], w=c["ones_r16"][:, 0:128],
             r=bbr_s: e.matmul(o, w, r, start=False, stop=True), s_mm)
        b.wait("scalar", s_mm)
        b.op("scalar", lambda e, o=beta_tm[:, ti * 16:(ti + 1) * 16],
             s=bank[:, 0:16]: e.activation(o, s, AF.Sigmoid), s_cha)

    s_p = b.sem("p")
    s_pv = b.sem("pv")
    s_ob = b.sem("ob")
    wqp3 = wq_p.rearrange("p (j k) -> p j k", j=3)
    for pair in range(8):
        s_wp = b.sem(f"wp{pair}")
        if pair == 0:
            b.wait("gpsimd", sA)
            b.wait("gpsimd", s_cha)
        else:
            b.wait("gpsimd", s_ob)
        for j, wsrc in enumerate((wqT, wkT, wvT)):
            b.dma("gpsimd",
                  wqp3[:, j].rearrange("p (cc k) -> p cc k", cc=8),
                  wsrc[:, pair * 128:(pair + 1) * 128].rearrange("(cc p) k -> p cc k", p=128),
                  s_wp)
        b.wait("tensor", s_wp)
        if pair > 0:
            b.wait("tensor", s_ob)
        else:
            b.wait("tensor", s_cha)

        for ti in range(NT):
            for j in range(3):
                bank = pb[j]
                for cc2 in range(8):
                    b.op("tensor", lambda e, o=bank[:, 0:128],
                         w=xn_view(xn_fm, cc2, ti * 128, (ti + 1) * 128),
                         r=wqp3[:, j].rearrange("p (cc k) -> p cc k", cc=8)[:, cc2],
                         st=(cc2 == 0), sp=(cc2 == 7):
                         e.matmul(o, w, r, start=st, stop=sp),
                         s_mm if cc2 == 7 else None)
            b.wait("scalar", s_mm)
            for (j, dstf) in ((0, qh), (1, kh)):
                bank = pb[j]
                if True:
                    b.wait("scalar", s_ch)
                b.op("scalar", lambda e, o=sqv, s=bank[:, 0:128]:
                     e.activation(o, s, AF.Square), s_cha)
                b.wait("vector", s_cha)
                b.op("vector", lambda e, o=nrm2[:, 0:2],
                     s=sqv.rearrange("p (h k) -> p h k", k=64):
                     e.reduce_sum(o, s, AX.X), s_ch)
                b.wait("scalar", s_ch)
                b.op("scalar", lambda e, o=nrm2[:, 0:2], s=nrm2[:, 0:2]:
                     e.activation(o, s, AF.Sqrt), s_cha)
                b.wait("vector", s_cha)
                b.op("vector", lambda e, o=nrm2[:, 2:4], s=nrm2[:, 0:2]:
                     e.tensor_scalar(o, s, 1e-12, None, ALU.max), s_ch)
                b.wait("vector", s_ch)
                b.op("vector", lambda e, o=nrm2[:, 2:4], s=nrm2[:, 2:4]:
                     e.reciprocal(o, s), s_ch)
                b.wait("vector", s_ch)
                b.wait("vector", s_tr)
                b.op("vector", lambda e, o=st16.rearrange("p (h k) -> p h k", k=64),
                     a=bank[:, 0:128].rearrange("p (h k) -> p h k", k=64),
                     r=nrm2[:, 2:4].unsqueeze(2).broadcast_to([128, 2, 64]):
                     e.tensor_tensor(o, a, r, ALU.mult), s_ch)
                b.wait("tensor", s_ch)
                b.wait("tensor", s_ev)
                b.op("tensor", lambda e, o=pb[3].bitcast(BF16)[:, 0:128],
                     s=st16, idt=c["ident"]: e.transpose(o, s, idt), s_tr)
                b.wait("vector", s_tr)
                b.op("vector", lambda e, o=dstf[:, ti * 128:(ti + 1) * 128],
                     s=pb[3].bitcast(BF16)[:, 0:128]: e.tensor_copy(o, s), s_ev)
            b.wait("vector", s_mm)
            bview = beta_tm[:, ti * 16 + 2 * pair: ti * 16 + 2 * pair + 2]
            b.op("vector", lambda e, o=vh[:, ti * 128:(ti + 1) * 128].rearrange(
                "p (h k) -> p h k", k=64),
                 a=pb[2][:, 0:128].rearrange("p (h k) -> p h k", k=64),
                 r=bview.unsqueeze(2).broadcast_to([128, 2, 64]):
                 e.tensor_tensor(o, a, r, ALU.mult), s_ev)

        for hh in range(2):
            h0 = hh * 64
            for qc in range(NTC):
                nk = (qc + 1) * 4
                nblk = (nk + 15) // 16
                for blk in range(nblk):
                    kt0 = blk * 16
                    kt1 = min(kt0 + 16, nk)
                    ev_tok1 = {}
                    if blk > 0:
                        b.wait("vector", s_pv)
                    for kt in range(kt0, kt1):
                        kj = kt - kt0
                        bank = pb[kt % 2]
                        if kj >= 2:
                            b.wait("tensor", s_p, ev_tok1[kj - 2])
                        b.op("tensor", lambda e, o=bank,
                             w=kh[h0:h0 + 64, kt * 128:(kt + 1) * 128],
                             r=qh[h0:h0 + 64, qc * TC:(qc + 1) * TC]:
                             e.matmul(o, w, r, start=True, stop=True), s_mm)
                        dst = p_buf[:, kj * 512:(kj + 1) * 512]
                        b.wait("vector", s_mm)
                        if kt >= nk - 4:
                            r_ = kt - (nk - 4)
                            b.op("vector", lambda e, o=dst, s=bank,
                                 m=masks[:, r_ * 512:(r_ + 1) * 512]:
                                 e.tensor_tensor(o, s, m, ALU.mult), s_p)
                        else:
                            b.op("vector", lambda e, o=dst, s=bank:
                                 e.tensor_copy(o, s), s_p)
                        ev_tok1[kj] = s_p.n
                    b.wait("tensor", s_p)
                    b.wait("tensor", s_ob)
                    for kt in range(kt0, kt1):
                        kj = kt - kt0
                        b.op("tensor", lambda e, o=pb[3][0:64, :],
                             w=vh[:, kt * 128 + h0: kt * 128 + h0 + 64],
                             r=p_buf[:, kj * 512:(kj + 1) * 512],
                             st=(kt == 0), sp=(kt == nk - 1):
                             e.matmul(o, w, r, start=st, stop=sp),
                             s_pv if kt == kt1 - 1 else None)
                b.wait("vector", s_pv)
                dsto = o_all.rearrange("p (cc t) -> p cc t", cc=8)[
                    h0:h0 + 64, pair, qc * TC:(qc + 1) * TC]
                b.op("vector", lambda e, o=dsto, s=pb[3][0:64, :]:
                     e.tensor_copy(o, s), s_ob)

    pjw = wkc[:, 0:8 * 1024]
    s_wo = b.sem("wo")
    b.wait("gpsimd", s_ob)
    b.dma("gpsimd", pjw.rearrange("p (cc k) -> p cc k", cc=8),
          woT.rearrange("(cc p) k -> p cc k", p=128), s_wo)
    b.wait("tensor", s_wo)
    b.wait("tensor", s_ob)
    b.wait("gpsimd", s_init)
    for dc in range(8):
        for tc in range(NTC):
            bank = pb[tc % 2]
            b.wait("tensor", s_ev)
            for cc2 in range(8):
                b.op("tensor", lambda e, o=bank,
                     w=pjw.rearrange("p (cc k) -> p cc k", cc=8)[:, cc2, dc * 128:(dc + 1) * 128],
                     r=o_all.rearrange("p (cc t) -> p cc t", cc=8)[:, cc2, tc * TC:(tc + 1) * TC],
                     st=(cc2 == 0), sp=(cc2 == 7):
                     e.matmul(o, w, r, start=st, stop=sp), s_mm if cc2 == 7 else None)
            emit_out_accum(b, c, bank, xo_d, dc, tc, st32a, st32tr, pb[5],
                           s_ev, s_tr, s_st, s_acc, s_mm_wait=(s_mm, s_mm.n))
    attn_acc_total = s_acc.n

    if "ffn" not in parts:
        tap_map = {"o_all": o_all, "qh": qh, "kh": kh, "vh": vh,
                   "beta": beta_tm, "xn_fm": xn_fm}
        for sm in (s_ev, s_ob, s_acc, s_mm, sA, s_ch, s_cha):
            b.wait("gpsimd", sm)
        s_tap = b.sem("tap")
        for name in taps:
            ap = tap_map[name]
            out_d = nc.dram_tensor("tap_" + name, list(ap.shape), ap.dtype,
                                   kind="ExternalOutput").ap()
            b.dma("gpsimd", out_d, ap, s_tap)
        b.wait("gpsimd", s_tap)
        return b, x_d

    co = Cur(o_all)
    d_grp = co.take(8 * 2048)
    xn_g = co.take(8 * 1024)
    xstg = co.take(2048)
    cstg = co.take(2048)
    cw2 = Cur(wkc)
    phi_r = cw2.take(8 * 2048)
    lg_t = cw2.take(2048)
    e_t = cw2.take(2048)
    c_t = cw2.take(2048)
    sD = wk32[:, 1024:1088]
    sC = wk32[:, 1088:1120]
    mcol = wk32[:, 1120:1121]
    phb_s = cw2.take(2048)[0:1, :]
    s_f1 = b.sem("f1")
    s_xta = b.sem("xta")

    b.wait("gpsimd", s_acc, attn_acc_total)
    b.dma("gpsimd", phi_r.rearrange("p (cc k) -> p cc k", cc=8),
          phiT.rearrange("(cc p) k -> p cc k", p=128), s_f1)
    b.dma("gpsimd", phb_s, phib_d, s_f1)
    b.wait("tensor", s_f1)
    b.wait("tensor", s_acc, attn_acc_total)

    for g in range(4):
        for t8 in range(8):
            ti = g * 8 + t8
            for eN in range(4):
                bank = pb[eN % 2]
                b.wait("tensor", s_ch)
                for cc2 in range(8):
                    b.op("tensor", lambda e, o=bank,
                         w=xn_view(xn_fm, cc2, ti * 128, (ti + 1) * 128),
                         r=phi_r.rearrange("p (cc k) -> p cc k", cc=8)[:, cc2, eN * 512:(eN + 1) * 512],
                         st=(cc2 == 0):
                         e.matmul(o, w, r, start=st, stop=False))
                b.op("tensor", lambda e, o=bank, w=c["ones_r16"][:, 0:128],
                     r=phb_s[0:1, eN * 512:(eN + 1) * 512]:
                     e.matmul(o, w, r, start=False, stop=True), s_mm)
                b.wait("vector", s_mm)
                b.op("vector", lambda e, o=lg_t[:, eN * 512:(eN + 1) * 512], s=bank:
                     e.tensor_copy(o, s), s_ch)
            b.wait("vector", s_ch)
            b.op("vector", lambda e, o=mcol, s=lg_t: e.reduce_max(o, s, AX.X), s_ch)
            b.wait("vector", s_ch)
            b.op("vector", lambda e, o=mcol, s=mcol: e.tensor_scalar(
                o, s, -1.0, None, ALU.mult), s_ch)
            b.wait("scalar", s_ch)
            b.op("scalar", lambda e, o=e_t, s=lg_t, bm=mcol:
                 e.activation(o, s, AF.Exp, bias=bm), s_cha)
            b.wait("vector", s_cha)
            b.op("vector", lambda e, o=sD, s=e_t.rearrange("p (n q) -> p n q", q=32):
                 e.reduce_sum(o, s, AX.X), s_ch)
            b.op("vector", lambda e, o=sC, s=e_t.rearrange("p (n q) -> p q n", q=32):
                 e.reduce_sum(o, s, AX.X), s_ch)
            b.wait("vector", s_ch)
            b.op("vector", lambda e, o=sD, s=sD: e.reciprocal(o, s), s_ch)
            b.op("vector", lambda e, o=sC, s=sC: e.reciprocal(o, s), s_ch)
            b.wait("vector", s_ch)
            b.op("vector", lambda e, o=d_grp[:, t8 * 2048:(t8 + 1) * 2048].rearrange(
                "p (n q) -> p n q", q=32),
                 a=e_t.rearrange("p (n q) -> p n q", q=32),
                 r=sD.unsqueeze(2).broadcast_to([128, 64, 32]):
                 e.tensor_tensor(o, a, r, ALU.mult), s_ch)
            b.op("vector", lambda e, o=c_t.rearrange("p (n q) -> p q n", q=32),
                 a=e_t.rearrange("p (n q) -> p q n", q=32),
                 r=sC.unsqueeze(2).broadcast_to([128, 32, 64]):
                 e.tensor_tensor(o, a, r, ALU.mult), s_ch)
            b.wait("tensor", s_ch)
            b.wait("tensor", s_st)
            for g0 in (0, 8):
                for gj in range(8):
                    b.op("tensor", lambda e,
                         o=pb[4 + g0 // 8].bitcast(BF16)[:, gj * 128:(gj + 1) * 128],
                         s=c_t[:, (g0 + gj) * 128:(g0 + gj + 1) * 128],
                         idt=c["ident"]:
                         e.transpose(o, s, idt), s_tr if gj == 7 else None)
            b.wait("vector", s_tr)
            b.wait("vector", s_st)
            b.op("vector", lambda e, o=cstg.rearrange("p (cc k) -> p cc k", cc=16)[:, 0:8],
                 s=pb[4].bitcast(BF16).rearrange("p (cc k) -> p cc k", cc=8):
                 e.tensor_copy(o, s), s_st)
            b.op("vector", lambda e, o=cstg.rearrange("p (cc k) -> p cc k", cc=16)[:, 8:16],
                 s=pb[5].bitcast(BF16).rearrange("p (cc k) -> p cc k", cc=8):
                 e.tensor_copy(o, s), s_st)
            b.wait("gpsimd", s_st)
            b.dma("gpsimd", c_scr.rearrange("(cc p) t -> p cc t", p=128)[
                :, :, ti * 128:(ti + 1) * 128],
                cstg.rearrange("p (cc k) -> p cc k", cc=16), s_f1)
            b.wait("tensor", s_ev)
            for cc2 in range(8):
                b.op("tensor", lambda e, o=pb[6].bitcast(BF16)[:, cc2 * 128:(cc2 + 1) * 128],
                     s=xn_view(xn_fm, cc2, ti * 128, (ti + 1) * 128), idt=c["ident"]:
                     e.transpose(o, s, idt), s_tr if cc2 == 7 else None)
            b.wait("vector", s_tr)
            b.op("vector", lambda e, o=xn_g[:, t8 * 1024:(t8 + 1) * 1024].rearrange(
                "p (cc k) -> p cc k", cc=8),
                 s=pb[6].bitcast(BF16).rearrange("p (cc k) -> p cc k", cc=8):
                 e.tensor_copy(o, s), s_ev)
        b.wait("tensor", s_ev)
        b.wait("tensor", s_ch)
        for npc in range(16):
            for dN in range(2):
                bank = pb[dN]
                b.wait("tensor", s_xta)
                for t8 in range(8):
                    b.op("tensor", lambda e, o=bank,
                         w=d_grp[:, t8 * 2048 + npc * 128: t8 * 2048 + (npc + 1) * 128],
                         r=xn_g[:, t8 * 1024 + dN * 512: t8 * 1024 + (dN + 1) * 512],
                         st=(t8 == 0), sp=(t8 == 7):
                         e.matmul(o, w, r, start=st, stop=sp),
                         s_mm if t8 == 7 else None)
                b.wait("vector", s_mm)
                b.wait("vector", s_xta)
                b.op("vector", lambda e, o=xstg.bitcast(F32)[:, dN * 512:(dN + 1) * 512],
                     s=bank: e.tensor_copy(o, s), s_ch)
            b.wait("gpsimd", s_ch)
            dstx = xt_part[npc * 128:(npc + 1) * 128, :]
            if g == 0:
                b.dma("gpsimd", dstx, xstg.bitcast(F32), s_xta)
            else:
                b.dma("gpsimd", dstx, xstg.bitcast(F32), s_xta, accum_op=ALU.add)

    b.wait("gpsimd", s_xta)
    b.op("gpsimd", lambda e: e.collective_compute(
        "ReduceScatter", ALU.add, replica_groups=RG,
        ins=[xt_part], outs=[xt_red]), s_cc)
    b.wait("gpsimd", s_cc)

    cw3 = Cur(wkc)
    xr_t = [cw3.take(1024, f32=True) for _ in range(2)]
    xp_t = cw3.take(2048)
    xf_t = cw3.take(2048)
    ew_r = cw3.take(8 * 1024)
    y_loc = cw3.take(8 * 1024)
    ebr_s = wkc[0:1, 28672:29440].bitcast(F32)[0:1, 0:384] if False else wkc[0:1, 27392:29440].bitcast(F32)
    s_f2 = b.sem("f2")
    s_ew = b.sem("ew")
    b.wait("sync", s_cc)
    for r in range(2):
        b.dma("sync", xr_t[r], xt_red[r * 128:(r + 1) * 128, :], s_f2)
    b.wait("vector", s_f2)
    for r in range(2):
        b.op("vector", lambda e, o=xr_t[r], s=xr_t[r]:
             e.tensor_scalar(o, s, 65000.0, -33000.0, ALU.min, ALU.max), s_ch)
        b.wait("vector", s_ch)
        a = ln_a
        b.op("vector", lambda e, a=a, x_=xr_t[r]: e.reduce_sum(
            a[:, 1:2], x_, AX.X), s_ch)
        b.wait("scalar", s_ch)
        b.op("scalar", lambda e, o=e_t[:, 0:1024], x_=xr_t[r]: e.activation(
            o, x_, AF.Square), s_cha)
        b.wait("vector", s_cha)
        b.op("vector", lambda e, a=a, s=e_t[:, 0:1024]: e.reduce_sum(
            a[:, 0:1], s, AX.X), s_ch)
        b.wait("vector", s_ch)
        b.op("vector", lambda e, a=a: e.tensor_scalar(
            a[:, 2:3], a[:, 1:2], 1.0 / D, None, ALU.mult), s_ch)
        b.op("vector", lambda e, a=a: e.tensor_scalar(
            a[:, 3:4], a[:, 0:1], 1.0 / D, 1e-5, ALU.mult, ALU.add), s_ch)
        b.wait("vector", s_ch)
        b.op("vector", lambda e, a=a: e.tensor_tensor(
            a[:, 4:5], a[:, 2:3], a[:, 2:3], ALU.mult), s_ch)
        b.wait("vector", s_ch)
        b.op("vector", lambda e, a=a: e.tensor_tensor(
            a[:, 3:4], a[:, 3:4], a[:, 4:5], ALU.subtract), s_ch)
        b.wait("scalar", s_ch)
        b.op("scalar", lambda e, a=a: e.activation(a[:, 4:5], a[:, 3:4], AF.Sqrt), s_cha)
        b.wait("vector", s_cha)
        b.op("vector", lambda e, a=a: e.reciprocal(a[:, 5:6], a[:, 4:5]), s_ch)
        b.wait("vector", s_ch)
        b.op("vector", lambda e, a=a: e.tensor_tensor(
            a[:, 6:7], a[:, 2:3], a[:, 5:6], ALU.mult), s_ch)
        b.wait("vector", s_ch)
        b.op("vector", lambda e, a=a: e.tensor_scalar(
            a[:, 7:8], a[:, 6:7], -1.0, None, ALU.mult), s_ch)
        b.wait("vector", s_ch)
        b.op("vector", lambda e, o=e_t[:, 0:1024], x_=xr_t[r], a=a: e.tensor_scalar(
            o, x_, a[:, 5:6], a[:, 7:8], ALU.mult, ALU.add), s_ch)
        b.wait("scalar", s_ch)
        b.op("scalar", lambda e, o=xp_t[:, r * 1024:(r + 1) * 1024], s=e_t[:, 0:1024]:
             e.activation(o, s, AF.Tanh), s_cha)
        b.wait("tensor", s_cha)
        b.wait("tensor", s_ev)
        for cc2 in range(8):
            b.op("tensor", lambda e, o=pb[6].bitcast(BF16)[:, cc2 * 128:(cc2 + 1) * 128],
                 s=xp_t[:, r * 1024 + cc2 * 128: r * 1024 + (cc2 + 1) * 128],
                 idt=c["ident"]: e.transpose(o, s, idt), s_tr if cc2 == 7 else None)
        b.wait("vector", s_tr)
        b.op("vector", lambda e, o=xf_t.rearrange("p (cc k) -> p cc k", cc=8)[:, :, r * 128:(r + 1) * 128],
             s=pb[6].bitcast(BF16).rearrange("p (cc k) -> p cc k", cc=8):
             e.tensor_copy(o, s), s_ev)
    for el in range(8):
        b.wait("gpsimd", s_mm)
        b.dma("gpsimd", ew_r.rearrange("p (cc k) -> p cc k", cc=8),
              ew_d[el].rearrange("(cc p) k -> p cc k", p=128), s_ew)
        b.dma("gpsimd", ebr_s, eb_d[el:el + 1, :], s_ew)
        b.wait("tensor", s_ew)
        b.wait("tensor", s_ev)
        for hN in range(2):
            bank = pb[hN]
            b.wait("tensor", s_ch)
            for cc2 in range(8):
                b.op("tensor", lambda e, o=bank[0:32, :],
                     w=xf_t.rearrange("p (cc k) -> p cc k", cc=8)[:, cc2, el * 32:(el + 1) * 32],
                     r=ew_r.rearrange("p (cc k) -> p cc k", cc=8)[:, cc2, hN * 512:(hN + 1) * 512],
                     st=(cc2 == 0):
                     e.matmul(o, w, r, start=st, stop=False))
            b.op("tensor", lambda e, o=bank[0:32, :], w=c["ones_r32"][:, 0:32],
                 r=ebr_s[0:1, hN * 512:(hN + 1) * 512]:
                 e.matmul(o, w, r, start=False, stop=True), s_mm)
            b.wait("vector", s_mm)
            b.op("vector", lambda e, o=y_loc[0:32, el * 1024 + hN * 512: el * 1024 + (hN + 1) * 512],
                 s=bank[0:32, :]: e.tensor_copy(o, s), s_ch)
    s_yti = b.sem("yti")
    b.wait("gpsimd", s_ch)
    b.dma("gpsimd", yt_in.rearrange("(el p) h -> p el h", p=32),
          y_loc[0:32, :].rearrange("p (el h) -> p el h", el=8), s_yti)
    b.wait("gpsimd", s_yti)
    b.op("gpsimd", lambda e: e.collective_compute(
        "AllGather", ALU.bypass, replica_groups=RG,
        ins=[yt_in], outs=[yt_full]), s_cc)
    b.wait("gpsimd", s_cc)

    cw4 = Cur(o_all)
    yt_r = cw4.take(16 * 1024)
    cfm_b = [cw4.take(16 * 128) for _ in range(2)]
    ystg_b = [cw4.take(2048) for _ in range(2)]
    s_f3 = b.sem("f3")
    s_cl = b.sem("cl")
    s_ya = b.sem("ya")
    s_accp = [s_acc, s_ya]
    acc_tok = [0, 0]
    b.wait("gpsimd", s_ev)
    b.wait("gpsimd", s_st)
    b.dma("gpsimd", yt_r.rearrange("p (cc k) -> p cc k", cc=16),
          yt_full.rearrange("(cc p) k -> p cc k", p=128), s_f3)
    b.wait("tensor", s_f3)
    cl_tok = [0, 0]
    b.wait("sync", s_f1)
    for ti in range(NT):
        par = ti % 2
        bcf = cfm_b[par]
        ystg = ystg_b[par]
        b.wait("sync", s_mm, cl_tok[par])
        b.dma("sync", bcf.rearrange("p (cc k) -> p cc k", cc=16),
              c_scr[:, ti * 128:(ti + 1) * 128].rearrange("(cc p) k -> p cc k", p=128),
              s_cl)
        b.wait("tensor", s_cl)
        for hN in range(2):
            bank = pb[2 + hN]
            b.wait("tensor", s_ch)
            for cc2 in range(16):
                b.op("tensor", lambda e, o=bank,
                     w=bcf.rearrange("p (cc k) -> p cc k", cc=16)[:, cc2],
                     r=yt_r.rearrange("p (cc k) -> p cc k", cc=16)[:, cc2, hN * 512:(hN + 1) * 512],
                     st=(cc2 == 0), sp=(cc2 == 15):
                     e.matmul(o, w, r, start=st, stop=sp), s_mm if cc2 == 15 else None)
            b.wait("vector", s_mm)
            b.wait("vector", s_accp[par], acc_tok[par])
            b.op("vector", lambda e, o=ystg.bitcast(F32)[:, hN * 512:(hN + 1) * 512],
                 s=bank: e.tensor_copy(o, s), s_ch)
        cl_tok[par] = s_mm.n
        b.wait("gpsimd", s_ch)
        b.dma("gpsimd", xo_d[ti * 128:(ti + 1) * 128, :], ystg.bitcast(F32),
              s_accp[par], accum_op=ALU.add)
        acc_tok[par] = s_accp[par].n

    b.wait("gpsimd", s_acc)
    b.wait("gpsimd", s_ya)
    return b, x_d


def host_prep_l1(inputs, core):
    t = lambda a: np.ascontiguousarray(np.asarray(a, np.float32).T)
    n_ = lambda a: np.ascontiguousarray(np.asarray(a, np.float32))
    return {
        "wqT": n_(inputs["L1_attn_Wq"]), "wkT": n_(inputs["L1_attn_Wk"]),
        "wvT": n_(inputs["L1_attn_Wv"]), "woT": n_(inputs["L1_attn_Wo"]),
        "wb": np.asarray(inputs["L1_attn_Wb"], np.float32),
        "bb": np.asarray(inputs["L1_attn_bb"], np.float32).reshape(1, 16),
        "phiT": t(inputs["L1_ffn_phi_w"]),
        "phib": np.asarray(inputs["L1_ffn_phi_b"], np.float32).reshape(1, 2048),
        "ew": np.ascontiguousarray(np.asarray(inputs["L1_ffn_expert_w"], np.float32)[core * 8:(core + 1) * 8]),
        "eb": np.ascontiguousarray(np.asarray(inputs["L1_ffn_expert_b"], np.float32)[core * 8:(core + 1) * 8]),
    }


def run_l1(x_full, inputs):
    b, _ = build_l1()
    nc = b.finish()
    in_maps = []
    for core in range(8):
        m = host_prep_l1(inputs, core)
        m["x"] = np.ascontiguousarray(x_full[core], dtype=np.float32)
        in_maps.append(m)
    res = run_bass_kernel_spmd(nc, in_maps, core_ids=list(range(8)))
    return np.stack([res.results[i]["xo"] for i in range(8)], axis=0)



def build_fused():
    b = B()
    nc = b.nc
    arena = nc.alloc_sbuf_tensor("arena", [128, 62208], BF16)[:]
    b.pool["o_all"] = arena[:, 0:32768]
    b.pool["wk16"] = arena[:, 0:53248]
    b.pool["wkb"] = arena[:, 0:53504]
    b.pool["wka"] = arena[:, 32768:62208]
    b.pool["wkc"] = arena[:, 32768:62208]
    b.pool["wq_s"] = arena[:, 53248:54272]
    b.pool["wk_s"] = arena[:, 54272:55296]
    b.pool["wv_s"] = arena[:, 55296:56320]
    b.pool["wo_s"] = arena[:, 56320:57344]
    b.pool["w8T"] = arena[0:8, 53504:57600]
    b.pool["sc_l3cos"] = arena[:, 60160:61184]
    b.pool["sc_l3sin"] = arena[:, 61184:62208]

    x_d = nc.dram_tensor("x", [T, D], F32, kind="ExternalInput").ap()
    x1 = nc.dram_tensor("x1", [T, D], F32, kind="Internal").ap()
    x2 = nc.dram_tensor("x2", [T, D], F32, kind="Internal").ap()
    x3 = nc.dram_tensor("x3", [T, D], F32, kind="Internal").ap()
    xo = nc.dram_tensor("xo", [T, D], F32, kind="ExternalOutput").ap()

    build_l0(b=b, x_d=x_d, xo_d=x1, pfx="l0_")
    b.barrier()
    build_l1(b=b, x_d=x1, xo_d=x2, pfx="l1_")
    b.barrier()
    build_l2(b=b, x_d=x2, xo_d=x3, pfx="l2_")
    b.barrier()
    build_l3(b=b, x_d=x3, xo_d=xo, pfx="l3_")
    b.barrier()
    return b


def kernel(**inputs):
    x = np.ascontiguousarray(np.asarray(inputs["L0_attn_x"], dtype=np.float32))
    b = build_fused()
    nc = b.finish()
    w0 = host_prep_l0(inputs)
    w2 = host_prep_l2(inputs)
    w3 = host_prep_l3(inputs)
    in_maps = []
    for core in range(8):
        m = {}
        m["x"] = np.ascontiguousarray(x[core])
        for pfx, w in (("l0_", w0), ("l1_", host_prep_l1(inputs, core)),
                       ("l2_", w2), ("l3_", w3)):
            for k_, v_ in w.items():
                m[pfx + k_] = v_
        in_maps.append(m)
    res = run_bass_kernel_spmd(nc, in_maps, core_ids=list(range(8)))
    return np.stack([res.results[i]["xo"] for i in range(8)], axis=0)


def kernel_unfused(**inputs):
    x = np.ascontiguousarray(np.asarray(inputs["L0_attn_x"], dtype=np.float32))
    x = run_l0(x, inputs)
    x = run_l1(x, inputs)
    x = run_l2(x, inputs)
    x = run_l3(x, inputs)
    return x
